# version 01
# baseline (speedup 1.0000x reference)
import numpy as np
from contextlib import ExitStack
import concourse.bass as bass
import concourse.mybir as mybir
from concourse.bass_utils import run_bass_kernel_spmd

F32 = mybir.dt.float32
BF16 = mybir.dt.bfloat16
I32 = mybir.dt.int32
AF = mybir.ActivationFunctionType
ALU = mybir.AluOpType

T = 2048
HALO = 4
TW = T + HALO
NCH = 16
DM = 4096
KD = 32
EPS = 1e-5

H_A, P_A, G_A, N_A = 128, 64, 8, 128
INTER = H_A * P_A
CONVD = INTER + 2 * G_A * N_A
M_HBC = CONVD // 128
M_GATE = INTER // 128
H_B, KVH, D_B = 32, 8, 128

MAGIC1, MAGIC2 = 11, 22


class Sem:
    def __init__(self, nc, name):
        self.h = nc.alloc_semaphore(name)
        self.v = 0

    def inc(self, instr, n=1):
        instr.then_inc(self.h, n)
        self.v += n
        return self.v


class Ctx:
    def __init__(self, nc):
        self.nc = nc
        self._semi = 0
        self._phase_sems = []

    def sem(self, name=""):
        self._semi += 1
        sm = Sem(self.nc, f"s{self._semi}{name}")
        self._phase_sems.append(sm)
        return sm

    def release(self):
        if not self._phase_sems:
            return
        nc = self.nc
        sems = [sm.h for sm in self._phase_sems]
        self._phase_sems = []
        with nc.Block() as blk:
            @blk.gpsimd
            def _(g):
                nc.clear_and_free_semaphores(sems)


def ph_ln_stats(cx, src, W, m_sb, r_sb, ones_bf, PSb, tag, c_off=0):
    cx.release()
    nc = cx.nc
    es = ExitStack()
    xb = es.enter_context(nc.sbuf_tensor(f"lnx{tag}", [128, 2, W], BF16))
    xq = es.enter_context(nc.sbuf_tensor(f"lnq{tag}", [128, 2, W], BF16))
    ld = [cx.sem("ld0"), cx.sem("ld1")]; sq = cx.sem("sq"); mmd = cx.sem("mm")
    fins = cx.sem("fins"); finv = cx.sem("finv"); fin2 = cx.sem("fin2")
    ldv = {}
    ranges = []
    c = 0
    while c < W:
        w = min(1024, W - c)
        ranges.append((c, w))
        c += w
    NR = len(ranges)
    assert 4 * NR <= 8

    def psw(bk0, o, n):
        b0, c0w = bk0 + o // 512, o % 512
        if c0w + n <= 512:
            return PSb[:, b0, c0w:c0w + n]
        assert c0w == 0 and n % 512 == 0
        return PSb[:, b0:b0 + n // 512, :]

    def mean_ps(ri, o, n):
        return psw(2 * ri, o, n)

    def sq_ps(ri, o, n):
        return psw(2 * NR + 2 * ri, o, n)

    with nc.Block() as blk:
        @blk.gpsimd
        def _(g):
            for j in range(KD):
                if j >= 2:
                    g.wait_ge(mmd.h, j - 1)
                ldv[j] = ld[j % 2].inc(g.dma_start(out=xb[:, j % 2, 0:W], in_=src[j * 128:(j + 1) * 128, c_off:c_off + W]), 16)

        @blk.scalar
        def _(s):
            for j in range(KD):
                s.wait_ge(ld[j % 2].h, ldv[j])
                if j >= 2:
                    s.wait_ge(mmd.h, j - 1)
                sq.inc(s.activation(out=xq[:, j % 2, 0:W], in_=xb[:, j % 2, 0:W], func=AF.Square))

        @blk.tensor
        def _(t):
            for j in range(KD):
                t.wait_ge(sq.h, j + 1)
                last = None
                for ri, (c0, w) in enumerate(ranges):
                    for o in range(0, w, 512):
                        n = min(512, w - o)
                        t.matmul(out=mean_ps(ri, o, n), lhsT=ones_bf[:], rhs=xb[:, j % 2, c0 + o:c0 + o + n],
                                 start=(j == 0), stop=(j == KD - 1))
                        last = t.matmul(out=sq_ps(ri, o, n), lhsT=ones_bf[:], rhs=xq[:, j % 2, c0 + o:c0 + o + n],
                                        start=(j == 0), stop=(j == KD - 1))
                mmd.inc(last, 1)

        @blk.scalar
        def _(s):
            s.wait_ge(mmd.h, KD)
            for ri, (c0, w) in enumerate(ranges):
                fins.inc(s.activation(out=m_sb[:, c_off + c0:c_off + c0 + w], in_=mean_ps(ri, 0, w), func=AF.Copy, scale=1.0 / DM))
                s.wait_ge(fins.h, fins.v)
                fins.inc(s.activation(out=r_sb[:, c_off + c0:c_off + c0 + w], in_=m_sb[:, c_off + c0:c_off + c0 + w], func=AF.Square))

        @blk.vector
        def _(v):
            for ri, (c0, w) in enumerate(ranges):
                v.wait_ge(fins.h, 2 * (ri + 1))
                finv.inc(v.scalar_tensor_tensor(out=r_sb[:, c_off + c0:c_off + c0 + w], in0=sq_ps(ri, 0, w), scalar=1.0 / DM,
                                                in1=r_sb[:, c_off + c0:c_off + c0 + w], op0=ALU.mult, op1=ALU.subtract))

        @blk.scalar
        def _(s):
            for ri, (c0, w) in enumerate(ranges):
                s.wait_ge(finv.h, ri + 1)
                fin2.inc(s.activation(out=r_sb[:, c_off + c0:c_off + c0 + w], in_=r_sb[:, c_off + c0:c_off + c0 + w], func=AF.Ln, bias=EPS))
                s.wait_ge(fin2.h, fin2.v)
                fin2.inc(s.activation(out=r_sb[:, c_off + c0:c_off + c0 + w], in_=r_sb[:, c_off + c0:c_off + c0 + w], func=AF.Exp, scale=-0.5))
    es.close()


def ph_normalize(cx, src, c0, w, m_sb, r_sb, xln, tag):
    cx.release()
    nc = cx.nc
    es = ExitStack()
    xr = es.enter_context(nc.sbuf_tensor(f"nxr{tag}", [128, 2, w], F32))
    t1 = es.enter_context(nc.sbuf_tensor(f"nt1{tag}", [128, 2, w], F32))
    ld = [cx.sem("ld0"), cx.sem("ld1")]; dn = cx.sem("dn")
    ldv = {}
    with nc.Block() as blk:
        @blk.sync
        def _(sy):
            for j in range(KD):
                if j >= 2:
                    sy.wait_ge(dn.h, 2 * (j - 1))
                ldv[j] = ld[j % 2].inc(sy.dma_start(out=xr[:, j % 2, 0:w], in_=src[j * 128:(j + 1) * 128, c0:c0 + w]), 16)

        @blk.vector
        def _(v):
            for j in range(KD):
                v.wait_ge(ld[j % 2].h, ldv[j])
                dn.inc(v.tensor_tensor(out=t1[:, j % 2, 0:w], in0=xr[:, j % 2, 0:w], in1=m_sb[:, c0:c0 + w], op=ALU.subtract))
                v.wait_ge(dn.h, dn.v)
                dn.inc(v.tensor_tensor(out=xln[:, j, 0:w], in0=t1[:, j % 2, 0:w], in1=r_sb[:, c0:c0 + w], op=ALU.mult))
    es.close()


def ph_dequant(cx, wq_dram, scalesT, wffn_dram):
    cx.release()
    nc = cx.nc
    es = ExitStack()
    wf = es.enter_context(nc.sbuf_tensor("dq_wf", [128, 2, DM], F32))
    sr = es.enter_context(nc.sbuf_tensor("dq_sr", [128, 2, DM], F32))
    wc = es.enter_context(nc.sbuf_tensor("dq_wc", [128, 2, DM], F32))
    wb = es.enter_context(nc.sbuf_tensor("dq_wb", [128, 2, DM], BF16))
    ld = [cx.sem("ld0"), cx.sem("ld1")]; ca = cx.sem("ca"); mu = cx.sem("mu")
    st = [cx.sem("st0"), cx.sem("st1")]
    ldv = {}
    with nc.Block() as blk:
        @blk.gpsimd
        def _(g):
            for i in range(KD):
                if i >= 2:
                    g.wait_ge(mu.h, i - 1)
                ld[i % 2].inc(g.dma_start(out=wf[:, i % 2], in_=wq_dram[i * 128:(i + 1) * 128, :]), 16)
                ldv[i] = ld[i % 2].inc(g.dma_start(out=sr[:, i % 2], in_=scalesT[i * 128:(i + 1) * 128, :]), 16)

        @blk.scalar
        def _(s):
            for i in range(KD):
                s.wait_ge(ld[i % 2].h, ldv[i])
                if i >= 2:
                    s.wait_ge(mu.h, i - 1)
                ca.inc(s.activation(out=wc[:, i % 2], in_=wf[:, i % 2], func=AF.Copy, bias=-128.0))

        @blk.vector
        def _(v):
            for i in range(KD):
                v.wait_ge(ca.h, i + 1)
                if i >= 2:
                    v.wait_ge(st[i % 2].h, 16 * ((i - 2) // 2 + 1))
                mu.inc(v.tensor_tensor(out=wb[:, i % 2], in0=wc[:, i % 2], in1=sr[:, i % 2], op=ALU.mult))

        @blk.sync
        def _(sy):
            for i in range(KD):
                sy.wait_ge(mu.h, i + 1)
                st[i % 2].inc(sy.dma_start(out=wffn_dram[i * 128:(i + 1) * 128, :], in_=wb[:, i % 2]), 16)
            sy.wait_ge(st[0].h, st[0].v)
            sy.wait_ge(st[1].h, st[1].v)
    es.close()


def ph_proj_res(cx, W_dram, w_bf16, KC, MO, xln, w, x_prev, hc0, out_dram, PSb, tag, MG=2):
    cx.release()
    nc = cx.nc
    es = ExitStack()
    wbuf = es.enter_context(nc.sbuf_tensor(f"pjw{tag}", [128, 2, KC, MG * 128], BF16))
    res = es.enter_context(nc.sbuf_tensor(f"pjr{tag}", [128, 8, 512], F32))
    og = es.enter_context(nc.sbuf_tensor(f"pjo{tag}", [128, 8, 512], F32))
    assert MO % MG == 0
    ngrp = MO // MG
    ntt = (w + 511) // 512
    tiles = [(ti, ti * 512, min(512, w - ti * 512)) for ti in range(ntt)]
    NTOT = ngrp * ntt * MG
    wd = [cx.sem("wd0"), cx.sem("wd1")]; mmd = cx.sem("mm"); ev = cx.sem("ev")
    rld = [cx.sem(f"rl{r}") for r in range(4)]
    st = [cx.sem(f"st{r}") for r in range(8)]
    wdv = {}; rldv = {}; stv = {}

    def bank(gi, ti, m, n):
        return PSb[:, (gi % 2) * 4 + (ti % 2) * 2 + m, 0:n]

    def wsrc(gi):
        return bass.AP(W_dram.tensor, gi * MG * 128, [[MO * 128, 128], [128 * MO * 128, KC], [1, MG * 128]])

    with nc.Block() as blk:
        def wbody(eng):
            for gi in range(ngrp):
                if gi >= 2:
                    eng.wait_ge(mmd.h, (gi - 1) * ntt * MG)
                wdv[gi] = wd[gi % 2].inc(eng.dma_start(out=wbuf[:, gi % 2], in_=wsrc(gi)), 16)
        if w_bf16:
            @blk.scalar
            def _(s):
                wbody(s)
        else:
            @blk.gpsimd
            def _(g):
                wbody(g)

        @blk.tensor
        def _(t):
            bank_free = {}
            idx = 0
            for gi in range(ngrp):
                t.wait_ge(wd[gi % 2].h, wdv[gi])
                for ti, o, n in tiles:
                    for m in range(MG):
                        bk = (gi % 2) * 4 + (ti % 2) * 2 + m
                        if bk in bank_free:
                            t.wait_ge(ev.h, bank_free[bk])
                        last = None
                        for k in range(KC):
                            last = t.matmul(out=bank(gi, ti, m, n), lhsT=wbuf[:, gi % 2, k, m * 128:(m + 1) * 128],
                                            rhs=xln[:, k, o:o + n], start=(k == 0), stop=(k == KC - 1))
                        mmd.inc(last)
                        idx += 1
                        bank_free[bk] = idx

        @blk.sync
        def _(sy):
            idx = 0
            store_q = []
            for gi in range(ngrp):
                for ti, o, n in tiles:
                    for m in range(MG):
                        mo = gi * MG + m
                        if idx >= 4:
                            sy.wait_ge(ev.h, idx - 3)
                            smo, sti, so, sn, sidx = store_q.pop(0)
                            stv[sidx] = st[sidx % 8].inc(
                                sy.dma_start(out=out_dram[smo * 128:(smo + 1) * 128, hc0 + so:hc0 + so + sn],
                                             in_=og[:, sidx % 8, 0:sn]), 16)
                        rldv[idx] = rld[idx % 4].inc(
                            sy.dma_start(out=res[:, idx % 8, 0:n],
                                         in_=x_prev[mo * 128:(mo + 1) * 128, hc0 + o:hc0 + o + n]), 16)
                        store_q.append((mo, ti, o, n, idx))
                        idx += 1
            for (smo, sti, so, sn, sidx) in store_q:
                sy.wait_ge(ev.h, sidx + 1)
                stv[sidx] = st[sidx % 8].inc(
                    sy.dma_start(out=out_dram[smo * 128:(smo + 1) * 128, hc0 + so:hc0 + so + sn],
                                 in_=og[:, sidx % 8, 0:sn]), 16)
            for r in range(8):
                sy.wait_ge(st[r].h, st[r].v)

        @blk.vector
        def _(v):
            idx = 0
            for gi in range(ngrp):
                for ti, o, n in tiles:
                    for m in range(MG):
                        v.wait_ge(mmd.h, idx + 1)
                        v.wait_ge(rld[idx % 4].h, rldv[idx])
                        if idx >= 8:
                            v.wait_ge(st[(idx - 8) % 8].h, stv[idx - 8])
                        ev.inc(v.tensor_tensor(out=og[:, idx % 8, 0:n], in0=bank(gi, ti, m, n),
                                               in1=res[:, idx % 8, 0:n], op=ALU.add))
                        idx += 1
    es.close()


M_ALL_A = M_HBC + 1 + M_GATE


def ph_inproj_a(cx, WinT, xln, half, cvout, gate1, dtraw, cst, PSb):
    import os
    DO_CONV = os.environ.get("IP_NOCONV") is None
    DO_DIAG = os.environ.get("IP_NODIAG") is None
    """in_proj (hBC+dt+gate) for one half + depthwise conv + stashes."""
    cx.release()
    nc = cx.nc
    es = ExitStack()
    wbuf = es.enter_context(nc.sbuf_tensor(f"ipw{half}", [128, 2, KD, 256], BF16))
    convfull = es.enter_context(nc.sbuf_tensor(f"ipcf{half}", [128, 2, 1028], BF16))
    convodd = es.enter_context(nc.sbuf_tensor(f"ipco{half}", [128, 2, 1028], BF16))
    diag = es.enter_context(nc.sbuf_tensor(f"ipdg{half}", [128, 2, 4, 128], BF16))
    gstage = es.enter_context(nc.sbuf_tensor(f"ipgs{half}", [128, 4, 256], BF16))
    cstage = es.enter_context(nc.sbuf_tensor(f"ipcs{half}", [128, 4, 512], BF16))
    MO = M_ALL_A
    ngrp = (MO + 1) // 2
    if half == 0:
        tiles = [(0, 0, 260, 0, 0), (1, 260, 256, 256, 260), (2, 516, 256, 512, 516), (3, 772, 256, 768, 772)]
    else:
        tiles = [(ti, ti * 256, 256, 1024 + ti * 256, 4 + ti * 256) for ti in range(4)]
    wd = [cx.sem("wd0"), cx.sem("wd1")]; mm1 = cx.sem("m1"); ev1 = cx.sem("e1"); mm2 = cx.sem("m2")
    ev2 = cx.sem("e2"); db = cx.sem("db"); cc = cx.sem("cc"); cu = cx.sem("cu")
    stg = [cx.sem(f"sg{r}") for r in range(4)]; stc = [cx.sem(f"sc{r}") for r in range(4)]
    co = cx.sem("co")
    wdv = {}; stgv = {}; stcv = {}

    cum_chains = []
    tot = 0
    for gi in range(ngrp):
        tot += min(2, MO - gi * 2) * 4
        cum_chains.append(tot)
    ev1_of = {}
    sched = []
    idx = 0
    for gi in range(ngrp):
        nm = min(2, MO - gi * 2)
        for ml in range(nm):
            mg = gi * 2 + ml
            sched.append((gi, ml, mg))
            for (ti, *_r) in tiles:
                idx += 1
                ev1_of[(mg, ti)] = idx

    def wsrc(gi):
        nm = min(2, MO - gi * 2)
        return bass.AP(WinT.tensor, gi * 256, [[MO * 128, 128], [128 * MO * 128, KD], [1, nm * 128]])

    with nc.Block() as blk:
        @blk.gpsimd
        def _(g):
            for gi in range(ngrp):
                if gi >= 2:
                    g.wait_ge(mm1.h, cum_chains[gi - 2])
                wdv[gi] = wd[gi % 2].inc(g.dma_start(out=wbuf[:, gi % 2, :, 0:min(2, MO - gi * 2) * 128], in_=wsrc(gi)), 16)

        @blk.tensor
        def _(t):
            bank_free = {}
            idx1 = 0
            hbc = 0
            for (gi, ml, mg) in sched:
                if ml == 0:
                    t.wait_ge(wd[gi % 2].h, wdv[gi])
                for (ti, xc0, n, tok0, dcol) in tiles:
                    bk = 2 * (mg % 2) + ti % 2
                    if bk in bank_free:
                        t.wait_ge(*bank_free[bk])
                    last = None
                    for k in range(KD):
                        last = t.matmul(out=PSb[:, bk, 0:n], lhsT=wbuf[:, gi % 2, k, ml * 128:(ml + 1) * 128],
                                        rhs=xln[:, k, xc0:xc0 + n], start=(k == 0), stop=(k == KD - 1))
                    mm1.inc(last)
                    idx1 += 1
                    bank_free[bk] = (ev1.h, idx1)
                if mg < M_HBC and DO_CONV:
                    t.wait_ge(ev1.h, idx1)
                    t.wait_ge(co.h, hbc + 1)
                    if DO_DIAG:
                        t.wait_ge(db.h, (hbc + 1) * 4)
                    if half == 1:
                        t.wait_ge(cc.h, hbc + 1)
                    for ck in range(2):
                        cvbk = 4 + ck
                        if cvbk in bank_free:
                            t.wait_ge(*bank_free[cvbk])
                        last = None
                        for k in range(4):
                            off = ck * 512 + 1 + k
                            if off % 2 == 0:
                                rhs = convfull[:, mg % 2, off:off + 512]
                            else:
                                rhs = convodd[:, mg % 2, off - 1:off - 1 + 512]
                            last = t.matmul(out=PSb[:, cvbk, :], lhsT=(diag[:, mg % 2, k] if DO_DIAG else cst["ident_bf"][:]),
                                            rhs=rhs, start=(k == 0), stop=(k == 3))
                        mm2.inc(last)
                        bank_free[cvbk] = (ev2.h, 2 * hbc + ck + 1)
                    hbc += 1
                elif mg < M_HBC:
                    hbc += 1

        @blk.scalar
        def _(s):
            idx1 = 0
            hbc = 0
            gcount = 0
            e2 = 0
            for (gi, ml, mg) in sched:
                for (ti, xc0, n, tok0, dcol) in tiles:
                    bk = 2 * (mg % 2) + ti % 2
                    s.wait_ge(mm1.h, idx1 + 1)
                    if mg < M_HBC:
                        if DO_CONV and hbc >= 2 and ti == 0:
                            s.wait_ge(mm2.h, 2 * (hbc - 1))
                        ev1.inc(s.activation(out=convfull[:, mg % 2, dcol:dcol + n], in_=PSb[:, bk, 0:n], func=AF.Copy))
                    elif mg == M_HBC:
                        ev1.inc(s.activation(out=dtraw[:, half * 1028 + xc0:half * 1028 + xc0 + n],
                                             in_=PSb[:, bk, 0:n], func=AF.Identity, bias=cst["dtb"][:, 0:1]))
                    else:
                        if gcount >= 4:
                            s.wait_ge(stg[gcount % 4].h, 16 * ((gcount - 4) // 4 + 1))
                        off = 4 if (half == 0 and ti == 0) else 0
                        ev1.inc(s.activation(out=gstage[:, gcount % 4], in_=PSb[:, bk, off:off + 256], func=AF.Copy))
                        gcount += 1
                    idx1 += 1
                if mg < M_HBC and DO_CONV:
                    s.wait_ge(ev1.h, ev1_of[(mg, 3)])
                    if half == 1:
                        s.wait_ge(cc.h, hbc + 1)
                    co.inc(s.activation(out=convodd[:, mg % 2, 0:1027], in_=convfull[:, mg % 2, 1:1028], func=AF.Copy))
                    for ck in range(2):
                        s.wait_ge(mm2.h, 2 * hbc + ck + 1)
                        if e2 >= 4:
                            s.wait_ge(stc[e2 % 4].h, 16 * ((e2 - 4) // 4 + 1))
                        ev2.inc(s.activation(out=cstage[:, e2 % 4], in_=PSb[:, 4 + ck, :],
                                             func=AF.Silu, bias=cst["convb"][:, mg:mg + 1]))
                        e2 += 1
                    hbc += 1
                elif mg < M_HBC:
                    hbc += 1

        @blk.vector
        def _(v):
            for mg in range(M_HBC):
                if DO_DIAG and DO_CONV:
                    if mg >= 2:
                        v.wait_ge(mm2.h, 2 * (mg - 1))
                    for k in range(4):
                        db.inc(v.tensor_scalar(out=diag[:, mg % 2, k], in0=cst["ident_bf"][:],
                                               scalar1=cst["convw"][:, mg, k:k + 1], scalar2=None, op0=ALU.mult))
                if half == 1:
                    if not DO_CONV and mg >= 2:
                        v.wait_ge(ev1.h, ev1_of[(mg - 2, 3)])
                    cc.inc(v.tensor_copy(out=convfull[:, mg % 2, 1:4], in_=cst["carryH"][:, mg, 0:3]))
                if half == 0:
                    v.wait_ge(ev1.h, ev1_of[(mg, 3)])
                    cu.inc(v.tensor_copy(out=cst["carryH"][:, mg, 0:3], in_=convfull[:, mg % 2, 1025:1028]))

        @blk.sync
        def _(sy):
            gcount = 0
            e2 = 0
            hbc = 0
            for (gi, ml, mg) in sched:
                if mg < M_HBC and DO_CONV:
                    for ck in range(2):
                        sy.wait_ge(ev2.h, 2 * hbc + ck + 1)
                        stcv[e2] = stc[e2 % 4].inc(
                            sy.dma_start(out=cvout[mg, :, half * 1024 + ck * 512:half * 1024 + ck * 512 + 512],
                                         in_=cstage[:, e2 % 4]), 16)
                        e2 += 1
                    hbc += 1
                elif mg < M_HBC:
                    hbc += 1
                elif mg > M_HBC:
                    gj = mg - M_HBC - 1
                    for (ti, xc0, n, tok0, dcol) in tiles:
                        sy.wait_ge(ev1.h, ev1_of[(mg, ti)])
                        stgv[gcount] = stg[gcount % 4].inc(
                            sy.dma_start(out=gate1[gj, :, tok0:tok0 + 256], in_=gstage[:, gcount % 4]), 16)
                        gcount += 1
            for r in range(4):
                sy.wait_ge(stc[r].h, stc[r].v)
                sy.wait_ge(stg[r].h, stg[r].v)
    es.close()


def ph_dtpost_a(cx, dtraw, tb, cst, PSb, acb_dram, dtabT_dram, ekT_dram):
    cx.release()
    nc = cx.nc
    es = ExitStack()
    tmp1 = es.enter_context(nc.sbuf_tensor("dpt1", [128, T], F32))
    tmp2 = es.enter_context(nc.sbuf_tensor("dpt2", [128, T], F32))
    ttab = es.enter_context(nc.sbuf_tensor("dptt", [16, 2, 128], F32))
    sa = cx.sem("sa"); sv = cx.sem("sv"); sg = cx.sem("sgp"); se = cx.sem("se"); sev = cx.sem("sev"); st = cx.sem("st")
    with nc.Block() as blk:
        @blk.scalar
        def _(s):
            for h in range(2):
                sl = dtraw[:, (4 if h == 0 else 1028):(4 if h == 0 else 1028) + 1024]
                o = tmp1[:, h * 1024:(h + 1) * 1024]
                sa.inc(s.activation(out=o, in_=sl, func=AF.Abs))
                s.wait_ge(sa.h, sa.v)
                sa.inc(s.activation(out=o, in_=o, func=AF.Exp, scale=-1.0))
                s.wait_ge(sa.h, sa.v)
                sa.inc(s.activation(out=o, in_=o, func=AF.Ln, bias=1.0))
            sa.inc(s.activation(out=tb["acol"][:], in_=cst["alog"][:], func=AF.Exp))

        @blk.gpsimd
        def _(g):
            sg.inc(g.memset(tb["cumA"][:, 0:1], 0.0))

        @blk.vector
        def _(v):
            v.wait_ge(sa.h, 7)
            sv.inc(v.tensor_scalar(out=tb["acol"][:], in0=tb["acol"][:], scalar1=-1.0, scalar2=None, op0=ALU.mult))
            for h in range(2):
                sl = dtraw[:, (4 if h == 0 else 1028):(4 if h == 0 else 1028) + 1024]
                sv.inc(v.tensor_scalar(out=tmp2[:, h * 1024:(h + 1) * 1024], in0=sl, scalar1=0.0, scalar2=None, op0=ALU.max))
                v.wait_ge(sv.h, sv.v)
                sv.inc(v.tensor_tensor(out=tb["dt"][:, h * 1024:(h + 1) * 1024], in0=tmp2[:, h * 1024:(h + 1) * 1024],
                                       in1=tmp1[:, h * 1024:(h + 1) * 1024], op=ALU.add))
                v.wait_ge(sv.h, sv.v)
            sv.inc(v.tensor_scalar(out=tb["adt"][:], in0=tb["dt"][:], scalar1=tb["acol"][:, 0:1], scalar2=None, op0=ALU.mult))
            v.wait_ge(sv.h, sv.v)
            for c in range(NCH):
                sv.inc(v.tensor_tensor_scan(out=tb["acum"][:, c * 128:(c + 1) * 128],
                                            data0=tb["adt"][:, c * 128:(c + 1) * 128],
                                            data1=tb["adt"][:, c * 128:(c + 1) * 128],
                                            initial=0.0, op0=ALU.add, op1=ALU.bypass))
            v.wait_ge(sv.h, sv.v)
            for c in range(NCH):
                sv.inc(v.tensor_scalar(out=tb["dec"][:, c * 128:(c + 1) * 128], in0=tb["acum"][:, c * 128:(c + 1) * 128],
                                       scalar1=tb["acum"][:, c * 128 + 127:c * 128 + 128], scalar2=-1.0,
                                       op0=ALU.subtract, op1=ALU.mult))
            v.wait_ge(sv.h, sv.v)
            v.wait_ge(sg.h, 1)
            for c in range(NCH):
                sv.inc(v.tensor_tensor(out=tb["cumA"][:, c + 1:c + 2], in0=tb["cumA"][:, c:c + 1],
                                       in1=tb["acum"][:, c * 128 + 127:c * 128 + 128], op=ALU.add))
                v.wait_ge(sv.h, sv.v)

        @blk.scalar
        def _(s):
            s.wait_ge(sv.h, sv.v)
            for c in range(NCH):
                se.inc(s.activation(out=tb["dec"][:, c * 128:(c + 1) * 128], in_=tb["dec"][:, c * 128:(c + 1) * 128], func=AF.Exp))
            for c in range(NCH):
                se.inc(s.activation(out=tb["dtab"][:, c:c + 1], in_=tb["acum"][:, c * 128 + 127:c * 128 + 128], func=AF.Exp))
            se.inc(s.activation(out=tb["ek"][:], in_=tb["cumA"][:, 0:NCH], func=AF.Exp))

        @blk.vector
        def _(v):
            v.wait_ge(se.h, se.v)
            sev.inc(v.tensor_scalar(out=tb["ek"][:], in0=tb["ek"][:], scalar1=cst["corrmask"][:, 0:1], scalar2=None, op0=ALU.mult))

        @blk.sync
        def _(sy):
            sy.wait_ge(sv.h, sv.v)
            st.inc(sy.dma_start(out=acb_dram[:], in_=tb["acum"][:]), 16)
            sy.wait_ge(st.h, st.v)
    tq = cx.sem("tq"); te = cx.sem("te"); st2 = cx.sem("st2")
    tpsrc = [("dt", "dt_tm"), ("dec", "dec_tm"), ("acum", "ac_tm")]
    with nc.Block() as blk:
        @blk.tensor
        def _(t):
            t.wait_ge(se.h, se.v)
            t.wait_ge(sev.h, sev.v)
            cnt = 0
            for c in range(NCH):
                for (srcn, dstn) in tpsrc:
                    if cnt >= 2:
                        t.wait_ge(te.h, cnt - 1)
                    tq.inc(t.transpose(out=PSb[:, cnt % 2, 0:128],
                                       in_=tb[srcn][:, c * 128:(c + 1) * 128],
                                       identity=cst["identf"][:]))
                    cnt += 1
            for nm2 in ("dtab", "ek"):
                if cnt >= 2:
                    t.wait_ge(te.h, cnt - 1)
                tq.inc(t.transpose(out=PSb[0:16, cnt % 2, 0:128], in_=tb[nm2][:], identity=cst["identf"][:]))
                cnt += 1

        @blk.scalar
        def _(s):
            cnt = 0
            for c in range(NCH):
                for (srcn, dstn) in tpsrc:
                    s.wait_ge(tq.h, cnt + 1)
                    te.inc(s.activation(out=tb[dstn][:, c * 128:(c + 1) * 128], in_=PSb[:, cnt % 2, 0:128], func=AF.Copy))
                    cnt += 1
            for i2 in range(2):
                s.wait_ge(tq.h, cnt + 1)
                te.inc(s.activation(out=ttab[:, i2], in_=PSb[0:16, cnt % 2, 0:128], func=AF.Copy))
                cnt += 1

        @blk.sync
        def _(sy):
            sy.wait_ge(te.h, NCH * 3 + 2)
            st2.inc(sy.dma_start(out=dtabT_dram[:], in_=ttab[:, 0]), 16)
            st2.inc(sy.dma_start(out=ekT_dram[:], in_=ttab[:, 1]), 16)
            sy.wait_ge(st2.h, st2.v)
    es.close()


def ph_sprint_a(cx, cvout, xdt_dram, pref_dram, dtabT_dram, tb, cst, PSb):
    cx.release()
    nc = cx.nc
    es = ExitStack()
    cxb = es.enter_context(nc.sbuf_tensor("spcx", [128, 2, 72, 128], BF16))
    xdts = es.enter_context(nc.sbuf_tensor("spxd", [128, 64, 128], BF16))
    btm = es.enter_context(nc.sbuf_tensor("spbt", [128, 8, 128], BF16))
    bdec = es.enter_context(nc.sbuf_tensor("spbd", [128, 2, 128], BF16))
    dbc = es.enter_context(nc.sbuf_tensor("spdb", [128, 128], F32))
    dbrow = es.enter_context(nc.sbuf_tensor("spdr", [1, 2, 128], F32))
    ld = [cx.sem("ld0"), cx.sem("ld1")]; lr = [cx.sem("lr0"), cx.sem("lr1")]
    tpd = cx.sem("td"); tpx = cx.sem("tx")
    sm = cx.sem("sm"); dk = cx.sem("dk"); xe = cx.sem("xe"); bd = cx.sem("bd")
    ss = cx.sem("ss"); zz = cx.sem("zz"); pfP = [cx.sem("pfP0"), cx.sem("pfP1")]; pfX = cx.sem("pfX")
    S = tb["S"]

    def psbf(b):
        return PSb[:, b].bitcast(BF16)[:, 0:128]

    XE = 136

    with nc.Block() as blk:
        @blk.gpsimd
        def _(g):
            zz.inc(g.memset(S[:, 0], 0.0))
            for c in range(NCH):
                if c > 0:
                    g.wait_ge(ss.h, c * H_A)
                else:
                    g.wait_ge(zz.h, 1)
                pfP[c % 2].inc(g.dma_start(out=pref_dram[c], in_=S[:, c % 2]), 16)
                g.wait_ge(xe.h, c * XE + 128)
                pfX.inc(g.dma_start(out=xdt_dram[c], in_=xdts.rearrange("p a b -> p (a b)")), 16)
            g.wait_ge(pfP[0].h, pfP[0].v)
            g.wait_ge(pfP[1].h, pfP[1].v)
            g.wait_ge(pfX.h, pfX.v)

        @blk.sync
        def _(sy):
            for c in range(NCH):
                if c >= 2:
                    sy.wait_ge(tpx.h, (c - 1) * 72)
                    sy.wait_ge(tpd.h, c - 1)
                src = bass.AP(cvout.tensor, c * 128, [[2048, 128], [128 * 2048, 72], [1, 128]])
                ld[c % 2].inc(sy.dma_start(out=cxb[:, c % 2], in_=src), 16)
                lr[c % 2].inc(sy.dma_start(out=dbrow[0:1, c % 2], in_=dtabT_dram[c:c + 1, :]), 16)

        @blk.tensor
        def _(t):
            for c in range(NCH):
                t.wait_ge(ld[c % 2].h, 16 * (c // 2 + 1))
                t.wait_ge(lr[c % 2].h, 16 * (c // 2 + 1))
                if c > 0:
                    t.wait_ge(dk.h, c)
                tpd.inc(t.matmul(out=PSb[:, 4, 0:128], lhsT=cst["onesf"][0:1, :], rhs=dbrow[0:1, c % 2],
                                 start=True, stop=True))
                for jj in range(72):
                    gidx = c * 72 + jj
                    if gidx >= 2:
                        prev = gidx - 2
                        pc, pj = prev // 72, prev % 72
                        need = pc * XE + (2 * (pj + 1) if pj < 64 else 128 + (pj - 64) + 1)
                        t.wait_ge(xe.h, need)
                    tpx.inc(t.transpose(out=psbf(jj % 2), in_=cxb[:, c % 2, jj], identity=cst["ident_bf"][:]))
                for h in range(H_A):
                    t.wait_ge(bd.h, c * H_A + h + 1)
                    t.wait_ge(xe.h, c * XE + 2 * (h // 2 + 1))
                    if c * H_A + h >= 2:
                        t.wait_ge(ss.h, c * H_A + h - 1)
                    sm.inc(t.matmul(out=PSb[:, 6 + (h % 2), 0:64], lhsT=bdec[:, h % 2],
                                    rhs=xdts[:, h // 2, (h % 2) * 64:(h % 2) * 64 + 64],
                                    start=True, stop=True))

        @blk.scalar
        def _(s):
            for c in range(NCH):
                s.wait_ge(tpd.h, c + 1)
                if c > 0:
                    s.wait_ge(ss.h, c * H_A)
                dk.inc(s.activation(out=dbc[:], in_=PSb[:, 4, 0:128], func=AF.Copy))

        @blk.vector
        def _(v):
            for c in range(NCH):
                for j in range(64):
                    v.wait_ge(tpx.h, c * 72 + j + 1)
                    if j == 0:
                        v.wait_ge(sm.h, c * H_A)
                        v.wait_ge(pfX.h, 16 * c)
                    xe.inc(v.tensor_scalar(out=xdts[:, j, 0:64], in0=psbf(j % 2)[:, 0:64],
                                           scalar1=tb["dt_tm"][:, c * 128 + 2 * j:c * 128 + 2 * j + 1],
                                           scalar2=None, op0=ALU.mult))
                    xe.inc(v.tensor_scalar(out=xdts[:, j, 64:128], in0=psbf(j % 2)[:, 64:128],
                                           scalar1=tb["dt_tm"][:, c * 128 + 2 * j + 1:c * 128 + 2 * j + 2],
                                           scalar2=None, op0=ALU.mult))
                for g2 in range(8):
                    v.wait_ge(tpx.h, c * 72 + 64 + g2 + 1)
                    xe.inc(v.tensor_copy(out=btm[:, g2], in_=psbf((64 + g2) % 2)))
                for h in range(H_A):
                    if c * H_A + h >= 2:
                        v.wait_ge(sm.h, c * H_A + h - 1)
                    bd.inc(v.tensor_scalar(out=bdec[:, h % 2], in0=btm[:, h // 16],
                                           scalar1=tb["dec_tm"][:, c * 128 + h:c * 128 + h + 1],
                                           scalar2=None, op0=ALU.mult))
                    v.wait_ge(sm.h, c * H_A + h + 1)
                    v.wait_ge(dk.h, c + 1)
                    if c == 0 and h == 0:
                        v.wait_ge(zz.h, 1)
                    if c > 0 and h == 0:
                        v.wait_ge(pfP[(c - 1) % 2].h, 16 * ((c - 1) // 2 + 1))
                    ss.inc(v.scalar_tensor_tensor(out=S[:, (c + 1) % 2, h * 64:(h + 1) * 64],
                                                  in0=S[:, c % 2, h * 64:(h + 1) * 64],
                                                  scalar=dbc[:, h:h + 1], in1=PSb[:, 6 + (h % 2), 0:64],
                                                  op0=ALU.mult, op1=ALU.add))
    es.close()


def ph_bridge_fake(cx, src_ap, dst_sb, W, fi):
    cx.release()
    nc = cx.nc
    fslot = nc.dram_tensor(f"fkslot{fi}", [128, W], BF16).ap()
    st = cx.sem("fb")
    with nc.Block() as blk:
        @blk.gpsimd
        def _(g):
            st.inc(g.dma_start(out=fslot[:], in_=src_ap), 16)
            g.wait_ge(st.h, 16)
            st.inc(g.dma_start(out=dst_sb[:], in_=fslot[:]), 16)
            g.wait_ge(st.h, 32)


def ph_bridge(cx, src_ap, dst_sb, slot_dram, flag_dram, fi, magic):
    cx.release()
    nc = cx.nc
    es = ExitStack()
    ft = es.enter_context(nc.sbuf_tensor(f"brft{fi}", [1, 1], I32))
    flt = es.enter_context(nc.sbuf_tensor(f"brfl{fi}", [1, 2], I32))
    st = nc.alloc_semaphore(f"brs{fi}")
    s2 = nc.alloc_semaphore(f"brr{fi}")
    with nc.Block() as blk:
        @blk.gpsimd
        def _(g):
            g.memset(ft[:], magic)
            g.memset(flt[:], 0)
            pid = g.partition_id()
            parity = pid % 2
            for body in g.Switch(parity, 2):
                g.dma_start(out=slot_dram[body], in_=src_ap).then_inc(st, 16)
                g.wait_ge(st, 16)
                g.dma_start(out=flag_dram[0:1, 2 * fi + body:2 * fi + body + 1], in_=ft[:]).then_inc(st, 16)
            g.wait_ge(st, 32)
            g.dma_start(out=flt[:], in_=flag_dram[0:1, 2 * fi:2 * fi + 2]).then_inc(st, 16)
            with g.Fori(0, 800) as _i:
                v0 = g.value_load(flt[0:1, 0:1])
                v1 = g.value_load(flt[0:1, 1:2])
                with g.If((v0 < magic) | (v1 < magic)):
                    g.nop(cycle_cnt=4000)
                    g.dma_start(out=flt[:], in_=flag_dram[0:1, 2 * fi:2 * fi + 2]).then_inc(st, 16)
            for body in g.Switch(parity, 2):
                g.dma_start(out=dst_sb[:], in_=slot_dram[1 - body]).then_inc(s2, 16)
            g.wait_ge(s2, 16)
    es.close()


def ph_p2a(cx, cvout, gate1, pref_dram, xdt_dram, acb_dram, ekT_dram, ygate_dram, S_part, tb, cst, PSb):
    cx.release()
    nc = cx.nc
    F16 = mybir.dt.float16
    es = ExitStack()
    cvb = es.enter_context(nc.sbuf_tensor("p2cv", [128, 80, 128], BF16))
    gtb = es.enter_context(nc.sbuf_tensor("p2gt", [128, 64, 128], BF16))
    prb = es.enter_context(nc.sbuf_tensor("p2pr", [128, INTER], BF16))
    xdb = es.enter_context(nc.sbuf_tensor("p2xd", [128, 64, 128], BF16))
    acf = es.enter_context(nc.sbuf_tensor("p2ac", [1, 2, 2048], F32))
    ekr = es.enter_context(nc.sbuf_tensor("p2ek", [1, 2, 128], F32))
    stru = es.enter_context(nc.sbuf_tensor("p2st", [128, INTER], BF16))
    ekbc = es.enter_context(nc.sbuf_tensor("p2eb", [128, 128], F32))
    gms = es.enter_context(nc.sbuf_tensor("p2gm", [128, 8, 128], BF16))
    esc = es.enter_context(nc.sbuf_tensor("p2es", [128, 2, 128], F32))
    Esb = es.enter_context(nc.sbuf_tensor("p2E", [128, 2, 128], BF16))
    eac = es.enter_context(nc.sbuf_tensor("p2ea", [128, 2, 128], BF16))
    csc = es.enter_context(nc.sbuf_tensor("p2cs", [128, 2, 128], BF16))
    mss = es.enter_context(nc.sbuf_tensor("p2ms", [128, 2, 128], BF16))
    ysb = es.enter_context(nc.sbuf_tensor("p2y", [128, 64, 128], F16))
    ysq = es.enter_context(nc.sbuf_tensor("p2yq", [128, 2, 128], BF16))
    rbc = es.enter_context(nc.sbuf_tensor("p2rb", [128, 128], F32))
    gsl = es.enter_context(nc.sbuf_tensor("p2gs", [128, 2, 128], F32))
    tgt = es.enter_context(nc.sbuf_tensor("p2tg", [128, 2, 128], F32))
    ygs = es.enter_context(nc.sbuf_tensor("p2yg", [128, 64, 128], BF16))

    ldc = cx.sem("ldc"); ldg = cx.sem("ldg"); ldp = cx.sem("ldp"); ldx = cx.sem("ldx")
    lda = [cx.sem("lda0"), cx.sem("lda1")]; lde = cx.sem("lde")
    ebm = cx.sem("ebm"); ebe = cx.sem("ebe"); stu = cx.sem("stu")
    gmm = cx.sem("gmm"); gme = cx.sem("gme")
    k1m = cx.sem("k1m"); tsc = cx.sem("tsc"); exE = cx.sem("exE"); exA = cx.sem("exA")
    msd = cx.sem("msd"); cscd = cx.sem("csc")
    ymm = cx.sem("ymm"); yev = cx.sem("yev")
    rsm = cx.sem("rsm"); sqd = cx.sem("sqd"); rln = cx.sem("rln")
    sil = cx.sem("sil"); ynd = cx.sem("ynd"); ygd = cx.sem("ygd"); stY = cx.sem("stY")

    with nc.Block() as blk:
        @blk.sync
        def _(sy):
            for c in range(NCH):
                if c >= 1:
                    sy.wait_ge(yev.h, c * 64)
                    sy.wait_ge(cscd.h, c * H_A)
                    sy.wait_ge(stu.h, c * H_A)
                    sy.wait_ge(ymm.h, c * H_A)
                    sy.wait_ge(sil.h, c * 64)
                if c >= 2:
                    sy.wait_ge(ebm.h, c - 1)
                src = bass.AP(cvout.tensor, c * 128, [[2048, 128], [128 * 2048, 80], [1, 128]])
                ldc.inc(sy.dma_start(out=cvb[:], in_=src), 16)
                ldx.inc(sy.dma_start(out=xdb[:], in_=xdt_dram[c].rearrange("p (a b) -> p a b", a=64)), 16)
                ldp.inc(sy.dma_start(out=prb[:], in_=pref_dram[c]), 16)
                gsrc = bass.AP(gate1.tensor, c * 128, [[2048, 128], [128 * 2048, 64], [1, 128]])
                ldg.inc(sy.dma_start(out=gtb[:], in_=gsrc), 16)
                lde.inc(sy.dma_start(out=ekr[0:1, c % 2], in_=ekT_dram[c:c + 1, :]), 16)
                for q in range(8):
                    Q = c * 8 + q
                    if Q >= 2:
                        sy.wait_ge(k1m.h, 16 * (Q - 1))
                    asrc = bass.AP(acb_dram.tensor, 16 * q * 2048 + c * 128, [[1, 1], [2048, 16], [1, 128]])
                    lda[Q % 2].inc(sy.dma_start(out=acf[0:1, q % 2], in_=asrc), 16)
                sy.wait_ge(ygd.h, (c + 1) * 64)
                ysrc = bass.AP(ygate_dram.tensor, c * 128, [[2048, 128], [128 * 2048, 64], [1, 128]])
                stY.inc(sy.dma_start(out=ysrc, in_=ygs.rearrange("p a b -> p (a b)")), 16)
            sy.wait_ge(stY.h, stY.v)

        @blk.tensor
        def _(t):
            for c in range(NCH):
                t.wait_ge(lde.h, 16 * (c + 1))
                if c > 0:
                    t.wait_ge(ebe.h, c)
                ebm.inc(t.matmul(out=PSb[:, 7, 0:128], lhsT=cst["onesf"][0:1, :], rhs=ekr[0:1, c % 2],
                                 start=True, stop=True))
                t.wait_ge(ldc.h, 16 * (c + 1))
                for g2 in range(8):
                    if c * 8 + g2 >= 2:
                        t.wait_ge(gme.h, c * 8 + g2 - 1)
                    gmm.inc(t.matmul(out=PSb[:, 2 + g2 % 2, 0:128], lhsT=cvb[:, 64 + g2],
                                     rhs=cvb[:, 72 + g2], start=True, stop=True))
                t.wait_ge(ldx.h, 16 * (c + 1))
                for h in range(H_A):
                    t.wait_ge(lda[(c * 8 + h // 16) % 2].h, 16 * ((c * 8 + h // 16) // 2 + 1))
                    if c * H_A + h >= 2:
                        t.wait_ge(exA.h, c * H_A + h - 1)
                    k1m.inc(t.matmul(out=PSb[:, h % 2, 0:128], lhsT=cst["onesf"][0:1, :],
                                     rhs=acf[0:1, (h // 16) % 2, 128 * (h % 16):128 * (h % 16) + 128],
                                     start=True, stop=True))
                    j = h // 2
                    side = h % 2
                    t.wait_ge(msd.h, c * H_A + h + 1)
                    t.wait_ge(cscd.h, c * H_A + h + 1)
                    t.wait_ge(stu.h, c * H_A + h + 1)
                    if c * 64 + j >= 2 and side == 0:
                        t.wait_ge(yev.h, c * 64 + j - 1)
                    pY = PSb[:, 4 + j % 2]
                    t.matmul(out=pY[side * 64:side * 64 + 64, 0:128], lhsT=xdb[:, j, side * 64:side * 64 + 64],
                             rhs=mss[:, h % 2], start=True, stop=False)
                    ymm.inc(t.matmul(out=pY[side * 64:side * 64 + 64, 0:128], lhsT=stru[:, h * 64:(h + 1) * 64],
                                     rhs=csc[:, h % 2], start=False, stop=True))
                for j in range(64):
                    t.wait_ge(sqd.h, c * 64 + j + 1)
                    if j == 0 and c > 0:
                        t.wait_ge(rln.h, 2 * c)
                    rsm.inc(t.matmul(out=PSb[:, 6, 0:128], lhsT=cst["ones_bf"][:], rhs=ysq[:, j % 2],
                                     start=(j == 0), stop=(j == 63)))

        @blk.scalar
        def _(s):
            for c in range(NCH):
                s.wait_ge(ebm.h, c + 1)
                ebe.inc(s.activation(out=ekbc[:], in_=PSb[:, 7, 0:128], func=AF.Copy))
                for h in range(H_A):
                    s.wait_ge(tsc.h, c * H_A + h + 1)
                    exE.inc(s.activation(out=Esb[:, h % 2], in_=esc[:, h % 2], func=AF.Exp))
                    exA.inc(s.activation(out=eac[:, h % 2], in_=PSb[:, h % 2, 0:128], func=AF.Exp))
                for j in range(64):
                    s.wait_ge(yev.h, c * 64 + j + 1)
                    if c * 64 + j >= 2:
                        s.wait_ge(rsm.h, c * 64 + j - 1)
                    sqd.inc(s.activation(out=ysq[:, j % 2], in_=ysb[:, j], func=AF.Square))
                s.wait_ge(rsm.h, (c + 1) * 64)
                rln.inc(s.activation(out=rbc[:], in_=PSb[:, 6, 0:128], func=AF.Ln, scale=1.0 / INTER, bias=EPS))
                s.wait_ge(rln.h, rln.v)
                rln.inc(s.activation(out=rbc[:], in_=rbc[:], func=AF.Exp, scale=-0.5))
                s.wait_ge(ldg.h, 16 * (c + 1))
                for j in range(64):
                    if c * 64 + j >= 2:
                        s.wait_ge(ygd.h, c * 64 + j - 1)
                    sil.inc(s.activation(out=gsl[:, j % 2], in_=gtb[:, j], func=AF.Silu))

        @blk.vector
        def _(v):
            for c in range(NCH):
                for g2 in range(8):
                    v.wait_ge(gmm.h, c * 8 + g2 + 1)
                    gme.inc(v.tensor_tensor(out=gms[:, g2], in0=PSb[:, 2 + g2 % 2, 0:128], in1=cst["mask01"][:], op=ALU.mult))
                v.wait_ge(ebe.h, c + 1)
                v.wait_ge(ldp.h, 16 * (c + 1))
                for h in range(H_A):
                    if c >= 1 and h == 0:
                        v.wait_ge(ymm.h, c * H_A)
                    stu.inc(v.scalar_tensor_tensor(out=stru[:, h * 64:(h + 1) * 64], in0=S_part[:, h * 64:(h + 1) * 64],
                                                   scalar=ekbc[:, h:h + 1], in1=prb[:, h * 64:(h + 1) * 64],
                                                   op0=ALU.mult, op1=ALU.add))
                for h in range(H_A):
                    g2 = h // 16
                    v.wait_ge(k1m.h, c * H_A + h + 1)
                    if c * H_A + h >= 2:
                        v.wait_ge(exE.h, c * H_A + h - 1)
                    tsc.inc(v.tensor_scalar(out=esc[:, h % 2], in0=PSb[:, h % 2, 0:128],
                                            scalar1=tb["ac_tm"][:, c * 128 + h:c * 128 + h + 1],
                                            scalar2=0.0, op0=ALU.subtract, op1=ALU.min))
                    v.wait_ge(exE.h, c * H_A + h + 1)
                    v.wait_ge(gme.h, c * 8 + g2 + 1)
                    if c * H_A + h >= 2:
                        v.wait_ge(ymm.h, c * H_A + h - 1)
                    msd.inc(v.tensor_tensor(out=mss[:, h % 2], in0=gms[:, g2], in1=Esb[:, h % 2], op=ALU.mult))
                    v.wait_ge(exA.h, c * H_A + h + 1)
                    cscd.inc(v.tensor_tensor(out=csc[:, h % 2], in0=cvb[:, 72 + g2], in1=eac[:, h % 2], op=ALU.mult))
                    if h % 2 == 1:
                        j = h // 2
                        v.wait_ge(ymm.h, c * H_A + 2 * (j + 1))
                        if c >= 1 and j == 0:
                            v.wait_ge(sqd.h, c * 64)
                        yev.inc(v.scalar_tensor_tensor(out=ysb[:, j], in0=cvb[:, j], scalar=cst["Dexp"][:, j:j + 1],
                                                       in1=PSb[:, 4 + j % 2, 0:128], op0=ALU.mult, op1=ALU.add))
                v.wait_ge(rln.h, 2 * (c + 1))
                for j in range(64):
                    v.wait_ge(sil.h, c * 64 + j + 1)
                    ynd.inc(v.scalar_tensor_tensor(out=tgt[:, j % 2], in0=ysb[:, j], scalar=cst["normwc"][:, j:j + 1],
                                                   in1=rbc[:], op0=ALU.mult, op1=ALU.mult))
                    v.wait_ge(ynd.h, ynd.v)
                    if c >= 1 and j == 0:
                        v.wait_ge(stY.h, 16 * c)
                    ygd.inc(v.tensor_tensor(out=ygs[:, j], in0=tgt[:, j % 2], in1=gsl[:, j % 2], op=ALU.mult))
    es.close()


M_B = 81
DST = H_B * D_B


def ph_proj_b(cx, W3T, xln, half, qs3, ks3, vs3, gate3, dtraw3, cst, PSb):
    cx.release()
    nc = cx.nc
    es = ExitStack()
    wbuf = es.enter_context(nc.sbuf_tensor(f"pbw{half}", [128, 2, KD, 256], BF16))
    rstg = es.enter_context(nc.sbuf_tensor(f"pbrs{half}", [128, 6, 256], BF16))
    pstg = es.enter_context(nc.sbuf_tensor(f"pbps{half}", [128, 6, 256], BF16))
    rt1 = es.enter_context(nc.sbuf_tensor(f"pbr1{half}", [128, 2, 256], F32))
    rt2 = es.enter_context(nc.sbuf_tensor(f"pbr2{half}", [128, 2, 256], F32))
    ngrp = (M_B + 1) // 2
    tiles = [(ti, ti * 256, half * 1024 + ti * 256) for ti in range(4)]
    wd = [cx.sem("wd0"), cx.sem("wd1")]
    mm1 = cx.sem("m1"); rp = cx.sem("rp"); ev = cx.sem("ev")
    stR = [cx.sem(f"sR{r}") for r in range(6)]
    stP = [cx.sem(f"sP{r}") for r in range(6)]
    wdv = {}; stRv = {}; stPv = {}

    info = []
    nr = npl = psi = 0
    cum_chains = []
    tot = 0
    for gi in range(ngrp):
        nm = min(2, M_B - gi * 2)
        tot += nm * 4
        cum_chains.append(tot)
        for ml in range(nm):
            mg = gi * 2 + ml
            typ = "rope" if mg < 40 else ("win" if mg == 80 else "plain")
            for (ti, xc0, tok0) in tiles:
                d = dict(gi=gi, ml=ml, mg=mg, ti=ti, xc0=xc0, tok0=tok0,
                         typ=typ, idx=len(info))
                if typ == "rope":
                    d["ri"] = nr; nr += 1
                else:
                    d["ei"] = npl; npl += 1
                    if typ == "plain":
                        d["si"] = psi; psi += 1
                info.append(d)

    def dst_of(mg):
        if mg < 32:
            return qs3, mg
        if mg < 40:
            return ks3, mg - 32
        if mg < 48:
            return vs3, mg - 40
        return gate3, mg - 48

    def wsrc(gi):
        nm = min(2, M_B - gi * 2)
        return bass.AP(W3T.tensor, gi * 256, [[M_B * 128, 128], [128 * M_B * 128, KD], [1, nm * 128]])

    with nc.Block() as blk:
        @blk.gpsimd
        def _(g):
            for gi in range(ngrp):
                if gi >= 2:
                    g.wait_ge(mm1.h, cum_chains[gi - 2])
                wdv[gi] = wd[gi % 2].inc(g.dma_start(out=wbuf[:, gi % 2, :, 0:min(2, M_B - gi * 2) * 128],
                                                     in_=wsrc(gi)), 16)

        @blk.tensor
        def _(t):
            bank_state = {}
            for d in info:
                if d["ml"] == 0 and d["ti"] == 0:
                    t.wait_ge(wd[d["gi"] % 2].h, wdv[d["gi"]])
                bk = 2 * (d["mg"] % 2) + d["ti"] % 2
                if bk in bank_state:
                    t.wait_ge(*bank_state[bk])
                last = None
                for k in range(KD):
                    last = t.matmul(out=PSb[:, bk, 0:256],
                                    lhsT=wbuf[:, d["gi"] % 2, k, d["ml"] * 128:(d["ml"] + 1) * 128],
                                    rhs=xln[:, k, d["xc0"]:d["xc0"] + 256],
                                    start=(k == 0), stop=(k == KD - 1))
                mm1.inc(last)
                if d["typ"] == "rope":
                    bank_state[bk] = (rp.h, 2 * d["ri"] + 1)
                else:
                    bank_state[bk] = (ev.h, d["ei"] + 1)

        @blk.vector
        def _(v):
            for d in info:
                if d["typ"] != "rope":
                    continue
                ri = d["ri"]
                bk = 2 * (d["mg"] % 2) + d["ti"] % 2
                v.wait_ge(mm1.h, d["idx"] + 1)
                if ri >= 6:
                    v.wait_ge(stR[ri % 6].h, 16 * (ri // 6))
                cs = cst["costab"][:, d["tok0"]:d["tok0"] + 256]
                sn = cst["sintab"][:, d["tok0"]:d["tok0"] + 256]
                v.tensor_tensor(out=rt1[:, ri % 2], in0=PSb[:, bk, 0:256], in1=cs, op=ALU.mult)
                v.tensor_tensor(out=rt2[0:64, ri % 2], in0=PSb[64:128, bk, 0:256], in1=sn[0:64], op=ALU.mult)
                rp.inc(v.tensor_tensor(out=rt2[64:128, ri % 2], in0=PSb[0:64, bk, 0:256], in1=sn[64:128], op=ALU.mult))
                v.wait_ge(rp.h, rp.v)
                rp.inc(v.tensor_tensor(out=rstg[:, ri % 6], in0=rt1[:, ri % 2], in1=rt2[:, ri % 2], op=ALU.add))

        @blk.scalar
        def _(s):
            for d in info:
                if d["typ"] == "rope":
                    continue
                s.wait_ge(mm1.h, d["idx"] + 1)
                bk = 2 * (d["mg"] % 2) + d["ti"] % 2
                if d["typ"] == "plain":
                    si = d["si"]
                    if si >= 6:
                        s.wait_ge(stP[si % 6].h, 16 * (si // 6))
                    ev.inc(s.activation(out=pstg[:, si % 6], in_=PSb[:, bk, 0:256], func=AF.Copy))
                else:
                    ev.inc(s.activation(out=dtraw3[0:32, d["tok0"]:d["tok0"] + 256],
                                        in_=PSb[0:32, bk, 0:256], func=AF.Identity,
                                        bias=cst["dtbb"][0:32, 0:1]))

        @blk.sync
        def _(sy):
            for d in info:
                if d["typ"] == "win":
                    continue
                dstt, row = dst_of(d["mg"])
                if d["typ"] == "rope":
                    ri = d["ri"]
                    sy.wait_ge(rp.h, 2 * (ri + 1))
                    stRv[ri] = stR[ri % 6].inc(
                        sy.dma_start(out=dstt[row, :, d["tok0"]:d["tok0"] + 256], in_=rstg[:, ri % 6]), 16)
                else:
                    si = d["si"]
                    sy.wait_ge(ev.h, d["ei"] + 1)
                    stPv[si] = stP[si % 6].inc(
                        sy.dma_start(out=dstt[row, :, d["tok0"]:d["tok0"] + 256], in_=pstg[:, si % 6]), 16)
            for r in range(6):
                sy.wait_ge(stR[r].h, stR[r].v)
                sy.wait_ge(stP[r].h, stP[r].v)
    es.close()


def ph_dtpost_b(cx, dtraw3, tb3, cst, PSb, gb3_dram, dtabT3_d, ekT3_d):
    cx.release()
    nc = cx.nc
    es = ExitStack()
    tmp1 = es.enter_context(nc.sbuf_tensor("bpt1", [32, T], F32))
    tmp2 = es.enter_context(nc.sbuf_tensor("bpt2", [32, T], F32))
    ttab = es.enter_context(nc.sbuf_tensor("bptt", [16, 2, 32], F32))
    sa = cx.sem("sa"); sv = cx.sem("sv"); sg = cx.sem("sgp"); se = cx.sem("se")
    sev = cx.sem("sev"); st = cx.sem("st")
    with nc.Block() as blk:
        @blk.scalar
        def _(s):
            sa.inc(s.activation(out=tmp1[:], in_=dtraw3[0:32, :], func=AF.Abs))
            s.wait_ge(sa.h, sa.v)
            sa.inc(s.activation(out=tmp1[:], in_=tmp1[:], func=AF.Exp, scale=-1.0))
            s.wait_ge(sa.h, sa.v)
            sa.inc(s.activation(out=tmp1[:], in_=tmp1[:], func=AF.Ln, bias=1.0))
            sa.inc(s.activation(out=tb3["acol"][0:32, :], in_=cst["alogb"][0:32, :], func=AF.Exp))

        @blk.gpsimd
        def _(g):
            sg.inc(g.memset(tb3["cumA"][0:32, 0:1], 0.0))

        @blk.vector
        def _(v):
            v.wait_ge(sa.h, 4)
            sv.inc(v.tensor_scalar(out=tb3["acol"][0:32, :], in0=tb3["acol"][0:32, :],
                                   scalar1=-1.0, scalar2=None, op0=ALU.mult))
            sv.inc(v.tensor_scalar(out=tmp2[:], in0=dtraw3[0:32, :], scalar1=0.0, scalar2=None, op0=ALU.max))
            v.wait_ge(sv.h, sv.v)
            sv.inc(v.tensor_tensor(out=tb3["dt"][0:32, :], in0=tmp2[:], in1=tmp1[:], op=ALU.add))
            v.wait_ge(sv.h, sv.v)
            sv.inc(v.tensor_scalar(out=tb3["adt"][0:32, :], in0=tb3["dt"][0:32, :],
                                   scalar1=tb3["acol"][0:32, 0:1], scalar2=None, op0=ALU.mult))
            v.wait_ge(sv.h, sv.v)
            for c in range(NCH):
                sv.inc(v.tensor_tensor_scan(out=tb3["acum"][0:32, c * 128:(c + 1) * 128],
                                            data0=tb3["adt"][0:32, c * 128:(c + 1) * 128],
                                            data1=tb3["adt"][0:32, c * 128:(c + 1) * 128],
                                            initial=0.0, op0=ALU.add, op1=ALU.bypass))
            v.wait_ge(sv.h, sv.v)
            for c in range(NCH):
                sv.inc(v.tensor_scalar(out=tb3["dec"][0:32, c * 128:(c + 1) * 128],
                                       in0=tb3["acum"][0:32, c * 128:(c + 1) * 128],
                                       scalar1=tb3["acum"][0:32, c * 128 + 127:c * 128 + 128], scalar2=-1.0,
                                       op0=ALU.subtract, op1=ALU.mult))
            v.wait_ge(sv.h, sv.v)
            v.wait_ge(sg.h, 1)
            for c in range(NCH):
                sv.inc(v.tensor_tensor(out=tb3["cumA"][0:32, c + 1:c + 2], in0=tb3["cumA"][0:32, c:c + 1],
                                       in1=tb3["acum"][0:32, c * 128 + 127:c * 128 + 128], op=ALU.add))
                v.wait_ge(sv.h, sv.v)

        @blk.scalar
        def _(s):
            s.wait_ge(sv.h, sv.v)
            for c in range(NCH):
                se.inc(s.activation(out=tb3["dec"][0:32, c * 128:(c + 1) * 128],
                                    in_=tb3["dec"][0:32, c * 128:(c + 1) * 128], func=AF.Exp))
            for c in range(NCH):
                se.inc(s.activation(out=tb3["dtab"][0:32, c:c + 1],
                                    in_=tb3["acum"][0:32, c * 128 + 127:c * 128 + 128], func=AF.Exp))
            se.inc(s.activation(out=tb3["ek"][0:32, :], in_=tb3["cumA"][0:32, 0:NCH], func=AF.Exp))

        @blk.vector
        def _(v):
            v.wait_ge(se.h, se.v)
            sev.inc(v.tensor_scalar(out=tb3["ek"][0:32, :], in0=tb3["ek"][0:32, :],
                                    scalar1=cst["corrmask"][0:32, 0:1], scalar2=None, op0=ALU.mult))

        @blk.sync
        def _(sy):
            sy.wait_ge(sv.h, sv.v)
            st.inc(sy.dma_start(out=gb3_dram[:], in_=tb3["acum"][0:32, :]), 16)
            sy.wait_ge(st.h, st.v)
    tq = cx.sem("tq"); te = cx.sem("te"); st2 = cx.sem("st2")
    tpsrc = [("dt", "dt_tm"), ("dec", "dec_tm"), ("acum", "ac_tm")]
    with nc.Block() as blk:
        @blk.tensor
        def _(t):
            t.wait_ge(se.h, se.v)
            t.wait_ge(sev.h, sev.v)
            cnt = 0
            for c in range(NCH):
                for (srcn, dstn) in tpsrc:
                    if cnt >= 2:
                        t.wait_ge(te.h, cnt - 1)
                    tq.inc(t.transpose(out=PSb[:, cnt % 2, 0:32],
                                       in_=tb3[srcn][0:32, c * 128:(c + 1) * 128],
                                       identity=cst["identf"][0:32, 0:32]))
                    cnt += 1
            for nm2 in ("dtab", "ek"):
                if cnt >= 2:
                    t.wait_ge(te.h, cnt - 1)
                tq.inc(t.transpose(out=PSb[0:16, cnt % 2, 0:32], in_=tb3[nm2][0:32, 0:NCH],
                                   identity=cst["identf"][0:32, 0:32]))
                cnt += 1

        @blk.scalar
        def _(s):
            cnt = 0
            for c in range(NCH):
                for (srcn, dstn) in tpsrc:
                    s.wait_ge(tq.h, cnt + 1)
                    te.inc(s.activation(out=tb3[dstn][:, c * 32:(c + 1) * 32],
                                        in_=PSb[:, cnt % 2, 0:32], func=AF.Copy))
                    cnt += 1
            for i2 in range(2):
                s.wait_ge(tq.h, cnt + 1)
                te.inc(s.activation(out=ttab[:, i2], in_=PSb[0:16, cnt % 2, 0:32], func=AF.Copy))
                cnt += 1

        @blk.sync
        def _(sy):
            sy.wait_ge(te.h, NCH * 3 + 2)
            st2.inc(sy.dma_start(out=dtabT3_d[:], in_=ttab[:, 0]), 16)
            st2.inc(sy.dma_start(out=ekT3_d[:], in_=ttab[:, 1]), 16)
            sy.wait_ge(st2.h, st2.v)
    es.close()


def ph_sprint_b(cx, ks3, vs3, xcs3, pref3, dtabT3_d, tb3, cst, PSb):
    cx.release()
    nc = cx.nc
    es = ExitStack()
    kvb = es.enter_context(nc.sbuf_tensor("sbkv", [128, 2, 16, 128], BF16))
    ktm = es.enter_context(nc.sbuf_tensor("sbkt", [128, 8, 128], BF16))
    vtm = es.enter_context(nc.sbuf_tensor("sbvt", [128, 8, 128], BF16))
    xcb = es.enter_context(nc.sbuf_tensor("sbxc", [128, 32, 128], BF16))
    kdc = es.enter_context(nc.sbuf_tensor("sbkd", [128, 2, 128], BF16))
    dbc = es.enter_context(nc.sbuf_tensor("sbdb", [128, 32], F32))
    dbrow = es.enter_context(nc.sbuf_tensor("sbdr", [1, 2, 32], F32))
    ld = [cx.sem("ld0"), cx.sem("ld1")]; lr = [cx.sem("lr0"), cx.sem("lr1")]
    tpd = cx.sem("td"); tpx = cx.sem("tx"); sm = cx.sem("sm"); dk = cx.sem("dk")
    xe = cx.sem("xe"); bd = cx.sem("bd"); ss = cx.sem("ss"); zz = cx.sem("zz")
    pfP = [cx.sem("pfP0"), cx.sem("pfP1")]; pfX = cx.sem("pfX")
    S = tb3["S"]
    XE = 48

    def psbf(b):
        return PSb[:, b].bitcast(BF16)[:, 0:128]

    with nc.Block() as blk:
        @blk.gpsimd
        def _(g):
            zz.inc(g.memset(S[:, 0], 0.0))
            for c in range(NCH):
                if c > 0:
                    g.wait_ge(ss.h, c * H_B)
                else:
                    g.wait_ge(zz.h, 1)
                pfP[c % 2].inc(g.dma_start(out=pref3[c], in_=S[:, c % 2]), 16)
                g.wait_ge(xe.h, c * XE + XE)
                pfX.inc(g.dma_start(out=xcs3[c], in_=xcb.rearrange("p a b -> p (a b)")), 16)
            g.wait_ge(pfP[0].h, pfP[0].v)
            g.wait_ge(pfP[1].h, pfP[1].v)
            g.wait_ge(pfX.h, pfX.v)

        @blk.sync
        def _(sy):
            for c in range(NCH):
                if c >= 2:
                    sy.wait_ge(tpx.h, (c - 1) * 16)
                    sy.wait_ge(tpd.h, c - 1)
                ksrc = bass.AP(ks3.tensor, c * 128, [[T, 128], [128 * T, 8], [1, 128]])
                ld[c % 2].inc(sy.dma_start(out=kvb[:, c % 2, 0:8], in_=ksrc), 16)
                vsrc = bass.AP(vs3.tensor, c * 128, [[T, 128], [128 * T, 8], [1, 128]])
                ld[c % 2].inc(sy.dma_start(out=kvb[:, c % 2, 8:16], in_=vsrc), 16)
                lr[c % 2].inc(sy.dma_start(out=dbrow[0:1, c % 2], in_=dtabT3_d[c:c + 1, :]), 16)

        @blk.tensor
        def _(t):
            for c in range(NCH):
                t.wait_ge(ld[c % 2].h, 32 * (c // 2 + 1))
                t.wait_ge(lr[c % 2].h, 16 * (c // 2 + 1))
                if c > 0:
                    t.wait_ge(dk.h, c)
                tpd.inc(t.matmul(out=PSb[:, 4, 0:32], lhsT=cst["onesf"][0:1, :], rhs=dbrow[0:1, c % 2],
                                 start=True, stop=True))
                for jj in range(16):
                    gidx = c * 16 + jj
                    if gidx >= 2:
                        prev = gidx - 2
                        t.wait_ge(xe.h, (prev // 16) * XE + prev % 16 + 1)
                    tpx.inc(t.transpose(out=psbf(jj % 2), in_=kvb[:, c % 2, jj], identity=cst["ident_bf"][:]))
                for h in range(H_B):
                    t.wait_ge(bd.h, c * H_B + h + 1)
                    if c * H_B + h >= 2:
                        t.wait_ge(ss.h, c * H_B + h - 1)
                    sm.inc(t.matmul(out=PSb[:, 6 + (h % 2), 0:128], lhsT=kdc[:, h % 2],
                                    rhs=xcb[:, h], start=True, stop=True))

        @blk.scalar
        def _(s):
            for c in range(NCH):
                s.wait_ge(tpd.h, c + 1)
                if c > 0:
                    s.wait_ge(ss.h, c * H_B)
                dk.inc(s.activation(out=dbc[:, 0:32], in_=PSb[:, 4, 0:32], func=AF.Copy))

        @blk.vector
        def _(v):
            for c in range(NCH):
                for jj in range(8):
                    v.wait_ge(tpx.h, c * 16 + jj + 1)
                    xe.inc(v.tensor_copy(out=ktm[:, jj], in_=psbf(jj % 2)))
                for jj in range(8):
                    v.wait_ge(tpx.h, c * 16 + 8 + jj + 1)
                    xe.inc(v.tensor_copy(out=vtm[:, jj], in_=psbf(jj % 2)))
                for h in range(H_B):
                    if h == 0:
                        v.wait_ge(xe.h, c * XE + 16)
                        v.wait_ge(sm.h, c * H_B)
                        v.wait_ge(pfX.h, 16 * c)
                    xe.inc(v.tensor_scalar(out=xcb[:, h], in0=vtm[:, h // 4],
                                           scalar1=tb3["dt_tm"][:, c * 32 + h:c * 32 + h + 1],
                                           scalar2=None, op0=ALU.mult))
                for h in range(H_B):
                    if c * H_B + h >= 2:
                        v.wait_ge(sm.h, c * H_B + h - 1)
                    bd.inc(v.tensor_scalar(out=kdc[:, h % 2], in0=ktm[:, h // 4],
                                           scalar1=tb3["dec_tm"][:, c * 32 + h:c * 32 + h + 1],
                                           scalar2=None, op0=ALU.mult))
                    v.wait_ge(sm.h, c * H_B + h + 1)
                    v.wait_ge(dk.h, c + 1)
                    if c == 0 and h == 0:
                        v.wait_ge(zz.h, 1)
                    if c > 0 and h == 0:
                        v.wait_ge(pfP[(c - 1) % 2].h, 16 * ((c - 1) // 2 + 1))
                    ss.inc(v.scalar_tensor_tensor(out=S[:, (c + 1) % 2, h * 128:(h + 1) * 128],
                                                  in0=S[:, c % 2, h * 128:(h + 1) * 128],
                                                  scalar=dbc[:, h:h + 1], in1=PSb[:, 6 + (h % 2), 0:128],
                                                  op0=ALU.mult, op1=ALU.add))
    es.close()


def ph_p2b(cx, qs3, ks3, xcs3, gate3, pref3, gb3_dram, ekT3_d, ygate3, S_part3, tb3, cst, PSb):
    cx.release()
    nc = cx.nc
    es = ExitStack()
    qb = es.enter_context(nc.sbuf_tensor("pbq", [128, 32, 128], BF16))
    kb = es.enter_context(nc.sbuf_tensor("pbk", [128, 8, 128], BF16))
    xcb = es.enter_context(nc.sbuf_tensor("pbxc", [128, 32, 128], BF16))
    gtb = es.enter_context(nc.sbuf_tensor("pbg", [128, 32, 128], BF16))
    prb = es.enter_context(nc.sbuf_tensor("pbpr", [128, DST], BF16))
    acf = es.enter_context(nc.sbuf_tensor("pbac", [1, 2, DST], F32))
    ekr = es.enter_context(nc.sbuf_tensor("pbek", [1, 2, 32], F32))
    stru = es.enter_context(nc.sbuf_tensor("pbst", [128, DST], BF16))
    ekbc = es.enter_context(nc.sbuf_tensor("pbeb", [128, 32], F32))
    scs = es.enter_context(nc.sbuf_tensor("pbsc", [128, 2, 128], BF16))
    esc = es.enter_context(nc.sbuf_tensor("pbes", [128, 2, 128], F32))
    Esb = es.enter_context(nc.sbuf_tensor("pbE", [128, 2, 128], BF16))
    egb = es.enter_context(nc.sbuf_tensor("pbeg", [128, 2, 128], BF16))
    mss = es.enter_context(nc.sbuf_tensor("pbms", [128, 2, 128], BF16))
    qgb = es.enter_context(nc.sbuf_tensor("pbqg", [128, 2, 128], BF16))
    ysq = es.enter_context(nc.sbuf_tensor("pbyq", [128, 2, 128], BF16))
    rbc = es.enter_context(nc.sbuf_tensor("pbrb", [128, 2, 128], F32))
    ynb = es.enter_context(nc.sbuf_tensor("pbyn", [128, 2, 128], F32))
    gsl = es.enter_context(nc.sbuf_tensor("pbgs", [128, 2, 128], F32))
    ygs = es.enter_context(nc.sbuf_tensor("pbyg", [128, 32, 128], BF16))

    ldq = cx.sem("ldq"); ldk = cx.sem("ldk"); ldx = cx.sem("ldx"); ldg = cx.sem("ldg")
    ldp = cx.sem("ldp"); lda = [cx.sem("lda0"), cx.sem("lda1")]
    lde = [cx.sem("lde0"), cx.sem("lde1")]
    ebm = cx.sem("ebm"); ebe = cx.sem("ebe"); stu = cx.sem("stu")
    k1m = cx.sem("k1m"); tsc = cx.sem("tsc"); exE = cx.sem("exE"); exG = cx.sem("exG")
    scm = cx.sem("scm"); sce = cx.sem("sce"); msd = cx.sem("msd"); qgd = cx.sem("qgd")
    ymm = cx.sem("ymm"); sqd = cx.sem("sqd"); rsm = cx.sem("rsm"); rln = cx.sem("rln")
    ynd = cx.sem("ynd"); sil = cx.sem("sil"); ygd = cx.sem("ygd"); stY = cx.sem("stY")

    with nc.Block() as blk:
        @blk.sync
        def _(sy):
            for c in range(NCH):
                if c >= 1:
                    sy.wait_ge(scm.h, c * H_B)
                    sy.wait_ge(qgd.h, c * H_B)
                    sy.wait_ge(ymm.h, c * H_B)
                    sy.wait_ge(sil.h, c * H_B)
                    sy.wait_ge(stu.h, c * H_B)
                if c >= 2:
                    sy.wait_ge(ebm.h, c - 1)
                    sy.wait_ge(k1m.h, H_B * (c - 1))
                qsrc = bass.AP(qs3.tensor, c * 128, [[T, 128], [128 * T, 32], [1, 128]])
                ldq.inc(sy.dma_start(out=qb[:], in_=qsrc), 16)
                ksrc = bass.AP(ks3.tensor, c * 128, [[T, 128], [128 * T, 8], [1, 128]])
                ldk.inc(sy.dma_start(out=kb[:], in_=ksrc), 16)
                ldx.inc(sy.dma_start(out=xcb[:], in_=xcs3[c].rearrange("p (a b) -> p a b", a=32)), 16)
                gsrc = bass.AP(gate3.tensor, c * 128, [[T, 128], [128 * T, 32], [1, 128]])
                ldg.inc(sy.dma_start(out=gtb[:], in_=gsrc), 16)
                ldp.inc(sy.dma_start(out=prb[:], in_=pref3[c]), 16)
                lde[c % 2].inc(sy.dma_start(out=ekr[0:1, c % 2], in_=ekT3_d[c:c + 1, :]), 16)
                asrc = bass.AP(gb3_dram.tensor, c * 128, [[1, 1], [T, 32], [1, 128]])
                lda[c % 2].inc(sy.dma_start(out=acf[0:1, c % 2], in_=asrc), 16)
                sy.wait_ge(ygd.h, (c + 1) * H_B)
                ysrc = bass.AP(ygate3.tensor, c * 128, [[T, 128], [128 * T, 32], [1, 128]])
                stY.inc(sy.dma_start(out=ysrc, in_=ygs.rearrange("p a b -> p (a b)")), 16)
            sy.wait_ge(stY.h, stY.v)

        @blk.tensor
        def _(t):
            for c in range(NCH):
                t.wait_ge(lde[c % 2].h, 16 * (c // 2 + 1))
                if c > 0:
                    t.wait_ge(ebe.h, c)
                ebm.inc(t.matmul(out=PSb[:, 7, 0:32], lhsT=cst["onesf"][0:1, :], rhs=ekr[0:1, c % 2],
                                 start=True, stop=True))
                t.wait_ge(ldq.h, 16 * (c + 1))
                t.wait_ge(ldk.h, 16 * (c + 1))
                t.wait_ge(ldx.h, 16 * (c + 1))
                t.wait_ge(lda[c % 2].h, 16 * (c // 2 + 1))
                for h in range(H_B):
                    i = c * H_B + h
                    if i >= 2:
                        t.wait_ge(exG.h, i - 1)
                    k1m.inc(t.matmul(out=PSb[:, h % 2, 0:128], lhsT=cst["onesf"][0:1, :],
                                     rhs=acf[0:1, c % 2, h * 128:(h + 1) * 128], start=True, stop=True))
                    if i >= 2:
                        t.wait_ge(sce.h, i - 1)
                    scm.inc(t.matmul(out=PSb[:, 2 + h % 2, 0:128], lhsT=kb[:, h // 4], rhs=qb[:, h],
                                     start=True, stop=True))
                    t.wait_ge(msd.h, i + 1)
                    t.wait_ge(qgd.h, i + 1)
                    t.wait_ge(stu.h, i + 1)
                    if i >= 2:
                        t.wait_ge(ynd.h, i - 1)
                    pY = PSb[:, 4 + h % 2]
                    t.matmul(out=pY[:, 0:128], lhsT=xcb[:, h], rhs=mss[:, h % 2], start=True, stop=False)
                    ymm.inc(t.matmul(out=pY[:, 0:128], lhsT=stru[:, h * 128:(h + 1) * 128],
                                     rhs=qgb[:, h % 2], start=False, stop=True))
                    t.wait_ge(sqd.h, i + 1)
                    if i >= 1:
                        t.wait_ge(rln.h, 2 * i)
                    rsm.inc(t.matmul(out=PSb[:, 6, 0:128], lhsT=cst["ones_bf"][:], rhs=ysq[:, h % 2],
                                     start=True, stop=True))

        @blk.scalar
        def _(s):
            for c in range(NCH):
                s.wait_ge(ebm.h, c + 1)
                if c >= 1:
                    s.wait_ge(stu.h, c * H_B)
                ebe.inc(s.activation(out=ekbc[:, 0:32], in_=PSb[:, 7, 0:32], func=AF.Copy))
                s.wait_ge(ldg.h, 16 * (c + 1))
                for h in range(H_B):
                    i = c * H_B + h
                    s.wait_ge(tsc.h, i + 1)
                    if i >= 2:
                        s.wait_ge(msd.h, i - 1)
                    exE.inc(s.activation(out=Esb[:, h % 2], in_=esc[:, h % 2], func=AF.Exp))
                    if i >= 2:
                        s.wait_ge(qgd.h, i - 1)
                    exG.inc(s.activation(out=egb[:, h % 2], in_=PSb[:, h % 2, 0:128], func=AF.Exp))
                    s.wait_ge(ymm.h, i + 1)
                    if i >= 2:
                        s.wait_ge(rsm.h, i - 1)
                    sqd.inc(s.activation(out=ysq[:, h % 2], in_=PSb[:, 4 + h % 2, 0:128], func=AF.Square))
                    s.wait_ge(rsm.h, i + 1)
                    if i >= 2:
                        s.wait_ge(ynd.h, i - 1)
                    rln.inc(s.activation(out=rbc[:, h % 2], in_=PSb[:, 6, 0:128], func=AF.Ln,
                                         scale=1.0 / D_B, bias=EPS))
                    s.wait_ge(rln.h, rln.v)
                    rln.inc(s.activation(out=rbc[:, h % 2], in_=rbc[:, h % 2], func=AF.Exp, scale=-0.5))
                    if i >= 2:
                        s.wait_ge(ygd.h, i - 1)
                    sil.inc(s.activation(out=gsl[:, h % 2], in_=gtb[:, h], func=AF.Silu))

        @blk.vector
        def _(v):
            for c in range(NCH):
                v.wait_ge(ebe.h, c + 1)
                v.wait_ge(ldp.h, 16 * (c + 1))
                for h in range(H_B):
                    if c >= 1 and h == 0:
                        v.wait_ge(ymm.h, c * H_B)
                    stu.inc(v.scalar_tensor_tensor(out=stru[:, h * 128:(h + 1) * 128],
                                                   in0=S_part3[:, h * 128:(h + 1) * 128],
                                                   scalar=ekbc[:, h:h + 1], in1=prb[:, h * 128:(h + 1) * 128],
                                                   op0=ALU.mult, op1=ALU.add))
                for h in range(H_B):
                    i = c * H_B + h
                    v.wait_ge(k1m.h, i + 1)
                    if i >= 2:
                        v.wait_ge(exE.h, i - 1)
                    tsc.inc(v.tensor_scalar(out=esc[:, h % 2], in0=PSb[:, h % 2, 0:128],
                                            scalar1=tb3["ac_tm"][:, c * 32 + h:c * 32 + h + 1],
                                            scalar2=0.0, op0=ALU.subtract, op1=ALU.min))
                    v.wait_ge(scm.h, i + 1)
                    sce.inc(v.tensor_tensor(out=scs[:, h % 2], in0=PSb[:, 2 + h % 2, 0:128],
                                            in1=cst["mask01"][:], op=ALU.mult))
                    v.wait_ge(exE.h, i + 1)
                    if i >= 2:
                        v.wait_ge(ymm.h, i - 1)
                    v.wait_ge(sce.h, sce.v)
                    msd.inc(v.tensor_tensor(out=mss[:, h % 2], in0=scs[:, h % 2], in1=Esb[:, h % 2], op=ALU.mult))
                    v.wait_ge(exG.h, i + 1)
                    qgd.inc(v.tensor_tensor(out=qgb[:, h % 2], in0=qb[:, h], in1=egb[:, h % 2], op=ALU.mult))
                    v.wait_ge(rln.h, 2 * (i + 1))
                    ynd.inc(v.scalar_tensor_tensor(out=ynb[:, h % 2], in0=PSb[:, 4 + h % 2, 0:128],
                                                   scalar=cst["gnormc"][:, 0:1], in1=rbc[:, h % 2],
                                                   op0=ALU.mult, op1=ALU.mult))
                    v.wait_ge(sil.h, i + 1)
                    v.wait_ge(ynd.h, ynd.v)
                    if c >= 1 and h == 0:
                        v.wait_ge(stY.h, 16 * c)
                    ygd.inc(v.tensor_tensor(out=ygs[:, h], in0=ynb[:, h % 2], in1=gsl[:, h % 2], op=ALU.mult))
    es.close()


def build(last="sl4", upto=99, fake_bridge=False, upto3=99):
    nc = bass.Bass("TRN2", target_bir_lowering=False)
    cx = Ctx(nc)

    ap = lambda name, shape, dt: nc.dram_tensor(name, shape, dt, kind="ExternalInput").ap()
    xT = ap("xT", [DM, TW], F32)
    wq_ffn = ap("wq_ffn", [DM, DM], I32)
    scalesT = ap("scalesT", [DM, DM], F32)
    WinT = ap("WinT", [DM, M_ALL_A * 128], F32)
    WoutT = ap("WoutT", [INTER, DM], F32)
    convw = ap("convw", [CONVD, 4], F32)
    convbT = ap("convbT", [128, M_HBC], F32)
    dtb = ap("dtb", [128, 1], F32)
    alog = ap("alog", [128, 1], F32)
    DexpI = ap("DexpI", [128, 64], F32)
    normwc = ap("normwc", [128, 64], F32)
    identf = ap("identf", [128, 128], F32)
    mask01I = ap("mask01I", [128, 128], F32)
    corrmask = ap("corrmask", [128, 1], F32)
    W3T = ap("W3T", [DM, M_B * 128], F32)
    WoT = ap("WoT", [DST, DM], F32)
    costabI = ap("costabI", [128, T], F32)
    sintabI = ap("sintabI", [128, T], F32)
    dtbbI = ap("dtbbI", [32, 1], F32)
    alogbI = ap("alogbI", [32, 1], F32)
    gnormI = ap("gnormI", [128, 1], F32)
    out = nc.dram_tensor("out", [DM, T], F32, kind="ExternalOutput").ap()

    wffn = nc.dram_tensor("wffn", [DM, DM], BF16).ap()
    cvout = nc.dram_tensor("cvout", [M_HBC, 128, T], BF16).ap()
    gate1 = nc.dram_tensor("gate1", [M_GATE, 128, T], BF16).ap()
    xdtd = nc.dram_tensor("xdtd", [NCH, 128, INTER], BF16).ap()
    pref1 = nc.dram_tensor("pref1", [NCH, 128, INTER], BF16).ap()
    acb = nc.dram_tensor("acb", [128, T], F32).ap()
    dtabT_d = nc.dram_tensor("dtabT_d", [NCH, 128], F32).ap()
    ekT_d = nc.dram_tensor("ekT_d", [NCH, 128], F32).ap()
    ygate1 = nc.dram_tensor("ygate1", [M_GATE, 128, T], BF16).ap()
    xs1 = nc.dram_tensor("xs1", [DM, T], F32).ap()
    br1s = nc.dram_tensor("br1s", [2, 128, INTER], BF16, addr_space="Shared").ap()
    brf = nc.dram_tensor("brf", [1, 4], I32, addr_space="Shared").ap()
    qs3 = nc.dram_tensor("qs3", [32, 128, T], BF16).ap()
    ks3 = nc.dram_tensor("ks3", [8, 128, T], BF16).ap()
    vs3 = nc.dram_tensor("vs3", [8, 128, T], BF16).ap()
    gate3 = nc.dram_tensor("gate3", [32, 128, T], BF16).ap()
    xcs3 = nc.dram_tensor("xcs3", [NCH, 128, DST], BF16).ap()
    pref3 = nc.dram_tensor("pref3", [NCH, 128, DST], BF16).ap()
    gb3 = nc.dram_tensor("gb3", [32, T], F32).ap()
    dtabT3_d = nc.dram_tensor("dtabT3", [NCH, 32], F32).ap()
    ekT3_d = nc.dram_tensor("ekT3", [NCH, 32], F32).ap()
    ygate3 = nc.dram_tensor("ygate3", [32, 128, T], BF16).ap()
    br2s = nc.dram_tensor("br2s", [2, 128, DST], BF16, addr_space="Shared").ap()

    ones_bf = nc.alloc_sbuf_tensor("ones_bf", [128, 128], BF16)
    onesf = nc.alloc_sbuf_tensor("onesf", [128, 128], F32)
    PSb = nc.psum_tensor("PSb", [128, 8, 512], F32).__enter__()

    CONSTS = [EPS, 1.0]
    cvals = nc.alloc_sbuf_tensor("cvals", [128, len(CONSTS)], F32)
    for i, cv in enumerate(CONSTS):
        nc.const_aps.register(F32, cv, cvals[:, i:i + 1])

    cst = {}
    cst["ident_bf"] = nc.alloc_sbuf_tensor("c_idb", [128, 128], BF16)
    cst["identf"] = nc.alloc_sbuf_tensor("c_idf", [128, 128], F32)
    cst["convw"] = nc.alloc_sbuf_tensor("c_cw", [128, M_HBC, 4], F32)
    cst["convb"] = nc.alloc_sbuf_tensor("c_cb", [128, M_HBC], F32)
    cst["dtb"] = nc.alloc_sbuf_tensor("c_dtb", [128, 1], F32)
    cst["alog"] = nc.alloc_sbuf_tensor("c_alog", [128, 1], F32)
    cst["Dexp"] = nc.alloc_sbuf_tensor("c_dexp", [128, 64], F32)
    cst["normwc"] = nc.alloc_sbuf_tensor("c_nwc", [128, 64], F32)
    cst["mask01"] = nc.alloc_sbuf_tensor("c_msk", [128, 128], F32)
    cst["corrmask"] = nc.alloc_sbuf_tensor("c_cm", [128, 1], F32)
    cst["carryH"] = nc.alloc_sbuf_tensor("c_carry", [128, M_HBC, 3], BF16)
    cst["dtbb"] = nc.alloc_sbuf_tensor("c_dtbb", [32, 1], F32)
    cst["alogb"] = nc.alloc_sbuf_tensor("c_alogb", [32, 1], F32)
    cst["gnormc"] = nc.alloc_sbuf_tensor("c_gn", [128, 1], F32)
    cst["onesf"] = onesf
    cst["ones_bf"] = ones_bf

    ld0 = cx.sem("ld0")
    ld0g = cx.sem("ld0g")
    with nc.Block() as blk:
        @blk.gpsimd
        def _(g):
            g.memset(ones_bf[:], 1.0)
            g.memset(onesf[:], 1.0)
            for i, cv in enumerate(CONSTS):
                g.memset(cvals[:, i:i + 1], cv)
            ld0g.inc(g.dma_start(out=cst["ident_bf"][:], in_=identf[:]), 16)

        @blk.sync
        def _(sy):
            ld0.inc(sy.dma_start(out=cst["identf"][:], in_=identf[:]), 16)
            ld0.inc(sy.dma_start(out=cst["convw"][:], in_=convw.rearrange("(m p) k -> p m k", p=128)), 16)
            ld0.inc(sy.dma_start(out=cst["convb"][:], in_=convbT[:]), 16)
            ld0.inc(sy.dma_start(out=cst["dtb"][:], in_=dtb[:]), 16)
            ld0.inc(sy.dma_start(out=cst["alog"][:], in_=alog[:]), 16)
            ld0.inc(sy.dma_start(out=cst["Dexp"][:], in_=DexpI[:]), 16)
            ld0.inc(sy.dma_start(out=cst["normwc"][:], in_=normwc[:]), 16)
            ld0.inc(sy.dma_start(out=cst["mask01"][:], in_=mask01I[:]), 16)
            ld0.inc(sy.dma_start(out=cst["corrmask"][:], in_=corrmask[:]), 16)
            ld0.inc(sy.dma_start(out=cst["dtbb"][:], in_=dtbbI[:]), 16)
            ld0.inc(sy.dma_start(out=cst["alogb"][:], in_=alogbI[:]), 16)
            ld0.inc(sy.dma_start(out=cst["gnormc"][:], in_=gnormI[:]), 16)
            sy.wait_ge(ld0.h, ld0.v)
            sy.wait_ge(ld0g.h, ld0g.v)

    x_tok = bass.AP(xT.tensor, HALO, [[TW, DM], [1, T]])
    sl1_out = out if last == "sl1" else xs1

    es1 = ExitStack()
    tb = {}
    tb["ac_tm"] = es1.enter_context(nc.sbuf_tensor("t_actm", [128, T], F32))
    esM2 = ExitStack()
    tb["dt_tm"] = esM2.enter_context(nc.sbuf_tensor("t_dttm", [128, T], F32))
    tb["dec_tm"] = esM2.enter_context(nc.sbuf_tensor("t_dectm", [128, T], F32))
    esD = ExitStack()
    dtraw = esD.enter_context(nc.sbuf_tensor("t_dtraw", [128, TW], F32))

    esM = ExitStack()
    m_sb = esM.enter_context(nc.sbuf_tensor("m_sb1", [128, TW], F32))
    r_sb = esM.enter_context(nc.sbuf_tensor("r_sb1", [128, TW], F32))
    ph_ln_stats(cx, xT, 2048, m_sb, r_sb, ones_bf, PSb, "1a")
    ph_ln_stats(cx, xT, HALO, m_sb, r_sb, ones_bf, PSb, "1b", c_off=2048)

    esx = ExitStack()
    xlnA = esx.enter_context(nc.sbuf_tensor("xlnA", [128, KD, 1028], BF16))
    for half in range(2):
        c0, w = (0, 1028) if half == 0 else (1028, 1024)
        ph_normalize(cx, xT, c0, w, m_sb, r_sb, xlnA, f"a{half}")
        if upto >= 1:
            ph_inproj_a(cx, WinT, xlnA, half, cvout, gate1, dtraw, cst, PSb)
    esx.close()
    esM.close()
    if upto < 2:
        return nc

    esT = ExitStack()
    tb["acol"] = esT.enter_context(nc.sbuf_tensor("t_acol", [128, 1], F32))
    tb["dt"] = esT.enter_context(nc.sbuf_tensor("t_dt", [128, T], F32))
    tb["adt"] = esT.enter_context(nc.sbuf_tensor("t_adt", [128, T], F32))
    tb["acum"] = esT.enter_context(nc.sbuf_tensor("t_acum", [128, T], F32))
    tb["dec"] = esT.enter_context(nc.sbuf_tensor("t_dec", [128, T], F32))
    tb["cumA"] = esT.enter_context(nc.sbuf_tensor("t_cumA", [128, NCH + 1], F32))
    tb["dtab"] = esT.enter_context(nc.sbuf_tensor("t_dtab", [128, NCH], F32))
    tb["ek"] = esT.enter_context(nc.sbuf_tensor("t_ek", [128, NCH], F32))
    ph_dtpost_a(cx, dtraw, tb, cst, PSb, acb, dtabT_d, ekT_d)
    esT.close()
    esD.close()
    if upto < 3:
        return nc

    esP = ExitStack()
    S_part = esP.enter_context(nc.sbuf_tensor("S_part", [128, INTER], BF16))
    esS = ExitStack()
    tb["S"] = esS.enter_context(nc.sbuf_tensor("t_S", [128, 2, INTER], F32))
    ph_sprint_a(cx, cvout, xdtd, pref1, dtabT_d, tb, cst, PSb)
    if upto < 4:
        return nc
    if fake_bridge:
        ph_bridge_fake(cx, tb["S"][:, 0], S_part, INTER, 0)
    else:
        ph_bridge(cx, tb["S"][:, 0], S_part, br1s, brf, 0, MAGIC1)
    esS.close()
    if upto < 5:
        return nc

    ph_p2a(cx, cvout, gate1, pref1, xdtd, acb, ekT_d, ygate1, S_part, tb, cst, PSb)
    esP.close()
    esM2.close()
    es1.close()
    if upto < 6:
        return nc

    for half in range(2):
        esY = ExitStack()
        ygr = esY.enter_context(nc.sbuf_tensor(f"ygr{half}", [128, M_GATE, 1024], BF16))
        ly = cx.sem("ly")
        with nc.Block() as blk:
            @blk.sync
            def _(sy):
                src = bass.AP(ygate1.tensor, half * 1024, [[T, 128], [128 * T, M_GATE], [1, 1024]])
                ly.inc(sy.dma_start(out=ygr[:], in_=src), 16)
                sy.wait_ge(ly.h, 16)
        ph_proj_res(cx, WoutT, False, M_GATE, KD, ygr, 1024, x_tok, half * 1024, sl1_out, PSb, f"op{half}", MG=1)
        esY.close()

    if last == "sl1":
        return nc

    ph_dequant(cx, wq_ffn, scalesT, wffn)
    xs2 = nc.dram_tensor("xs2", [DM, T], F32).ap()
    sl2_out = out if last == "sl2" else xs2
    esx2 = ExitStack()
    m_sb2 = esx2.enter_context(nc.sbuf_tensor("m_sb2", [128, T], F32))
    r_sb2 = esx2.enter_context(nc.sbuf_tensor("r_sb2", [128, T], F32))
    ph_ln_stats(cx, xs1, 2048, m_sb2, r_sb2, ones_bf, PSb, "2a")
    xlnF = esx2.enter_context(nc.sbuf_tensor("xlnF", [128, KD, 1024], BF16))
    for half in range(2):
        ph_normalize(cx, xs1, half * 1024, 1024, m_sb2, r_sb2, xlnF, f"f2{half}")
        ph_proj_res(cx, wffn, True, KD, KD, xlnF, 1024, xs1, half * 1024, sl2_out, PSb, f"f2{half}")
    esx2.close()
    if last == "sl2":
        return nc

    xs3 = nc.dram_tensor("xs3", [DM, T], F32).ap()
    sl3_out = out if last == "sl3" else xs3
    es31 = ExitStack()
    tb3 = {}
    tb3["ac_tm"] = es31.enter_context(nc.sbuf_tensor("t3_actm", [128, 512], F32))
    es3M2 = ExitStack()
    tb3["dt_tm"] = es3M2.enter_context(nc.sbuf_tensor("t3_dttm", [128, 512], F32))
    tb3["dec_tm"] = es3M2.enter_context(nc.sbuf_tensor("t3_dectm", [128, 512], F32))
    es3D = ExitStack()
    dtraw3 = es3D.enter_context(nc.sbuf_tensor("t3_dtraw", [32, T], F32))
    cst["costab"] = es3D.enter_context(nc.sbuf_tensor("c_cos", [128, T], F32))
    cst["sintab"] = es3D.enter_context(nc.sbuf_tensor("c_sin", [128, T], F32))
    lcs = cx.sem("lcs")
    with nc.Block() as blk:
        @blk.sync
        def _(sy):
            lcs.inc(sy.dma_start(out=cst["costab"][:], in_=costabI[:]), 16)
            lcs.inc(sy.dma_start(out=cst["sintab"][:], in_=sintabI[:]), 16)
            sy.wait_ge(lcs.h, lcs.v)

    es3M = ExitStack()
    m_sb3 = es3M.enter_context(nc.sbuf_tensor("m_sb3", [128, T], F32))
    r_sb3 = es3M.enter_context(nc.sbuf_tensor("r_sb3", [128, T], F32))
    ph_ln_stats(cx, xs2, 2048, m_sb3, r_sb3, ones_bf, PSb, "3a")
    es3x = ExitStack()
    xln3 = es3x.enter_context(nc.sbuf_tensor("xln3", [128, KD, 1024], BF16))
    for half in range(2):
        ph_normalize(cx, xs2, half * 1024, 1024, m_sb3, r_sb3, xln3, f"b{half}")
        if upto3 >= 1:
            ph_proj_b(cx, W3T, xln3, half, qs3, ks3, vs3, gate3, dtraw3, cst, PSb)
    es3x.close()
    es3M.close()
    if upto3 < 2:
        return nc

    es3T = ExitStack()
    tb3["acol"] = es3T.enter_context(nc.sbuf_tensor("t3_acol", [32, 1], F32))
    tb3["dt"] = es3T.enter_context(nc.sbuf_tensor("t3_dt", [32, T], F32))
    tb3["adt"] = es3T.enter_context(nc.sbuf_tensor("t3_adt", [32, T], F32))
    tb3["acum"] = es3T.enter_context(nc.sbuf_tensor("t3_acum", [32, T], F32))
    tb3["dec"] = es3T.enter_context(nc.sbuf_tensor("t3_dec", [32, T], F32))
    tb3["cumA"] = es3T.enter_context(nc.sbuf_tensor("t3_cumA", [32, NCH + 1], F32))
    tb3["dtab"] = es3T.enter_context(nc.sbuf_tensor("t3_dtab", [32, NCH], F32))
    tb3["ek"] = es3T.enter_context(nc.sbuf_tensor("t3_ek", [32, NCH], F32))
    ph_dtpost_b(cx, dtraw3, tb3, cst, PSb, gb3, dtabT3_d, ekT3_d)
    es3T.close()
    es3D.close()
    if upto3 < 3:
        return nc

    es3P = ExitStack()
    S_part3 = es3P.enter_context(nc.sbuf_tensor("S_part3", [128, DST], BF16))
    es3S = ExitStack()
    tb3["S"] = es3S.enter_context(nc.sbuf_tensor("t3_S", [128, 2, DST], F32))
    ph_sprint_b(cx, ks3, vs3, xcs3, pref3, dtabT3_d, tb3, cst, PSb)
    if upto3 < 4:
        return nc
    if fake_bridge:
        ph_bridge_fake(cx, tb3["S"][:, 0], S_part3, DST, 1)
    else:
        ph_bridge(cx, tb3["S"][:, 0], S_part3, br2s, brf, 1, MAGIC2)
    es3S.close()
    if upto3 < 5:
        return nc

    ph_p2b(cx, qs3, ks3, xcs3, gate3, pref3, gb3, ekT3_d, ygate3, S_part3, tb3, cst, PSb)
    es3P.close()
    es3M2.close()
    es31.close()
    if upto3 < 6:
        return nc

    for half in range(2):
        esY = ExitStack()
        ygr3 = esY.enter_context(nc.sbuf_tensor(f"ygr3{half}", [128, 32, 1024], BF16))
        ly = cx.sem("ly")
        with nc.Block() as blk:
            @blk.sync
            def _(sy):
                src = bass.AP(ygate3.tensor, half * 1024, [[T, 128], [128 * T, 32], [1, 1024]])
                ly.inc(sy.dma_start(out=ygr3[:], in_=src), 16)
                sy.wait_ge(ly.h, 16)
        ph_proj_res(cx, WoT, False, 32, KD, ygr3, 1024, xs2, half * 1024, sl3_out, PSb, f"o3{half}", MG=2)
        esY.close()

    if last == "sl3":
        return nc

    esx4 = ExitStack()
    m_sb4 = esx4.enter_context(nc.sbuf_tensor("m_sb4", [128, T], F32))
    r_sb4 = esx4.enter_context(nc.sbuf_tensor("r_sb4", [128, T], F32))
    ph_ln_stats(cx, xs3, 2048, m_sb4, r_sb4, ones_bf, PSb, "4a")
    xlnF4 = esx4.enter_context(nc.sbuf_tensor("xlnF4", [128, KD, 1024], BF16))
    for half in range(2):
        ph_normalize(cx, xs3, half * 1024, 1024, m_sb4, r_sb4, xlnF4, f"f4{half}")
        ph_proj_res(cx, wffn, True, KD, KD, xlnF4, 1024, xs3, half * 1024, out, PSb, f"f4{half}")
    esx4.close()
    return nc


def _prep_inputs(inp):
    x = np.asarray(inp["L0_attn_a_hidden_states"])
    wq = np.asarray(inp["L0_ffn_w_q"])
    scales = np.asarray(inp["L0_ffn_scales"])
    ipw = np.asarray(inp["L0_attn_a_in_proj_w"])
    convw = np.asarray(inp["L0_attn_a_conv_w"]).reshape(CONVD, 4)
    convb = np.asarray(inp["L0_attn_a_conv_b"])
    dtb = np.asarray(inp["L0_attn_a_dt_bias"])
    alog = np.asarray(inp["L0_attn_a_A_log"])
    Dv = np.asarray(inp["L0_attn_a_D"])
    normw = np.asarray(inp["L0_attn_a_norm_w"])
    wout = np.asarray(inp["L0_attn_a_out_proj_w"])

    Wq = np.asarray(inp["L0_attn_b_Wq"])
    Wk = np.asarray(inp["L0_attn_b_Wk"])
    Wv = np.asarray(inp["L0_attn_b_Wv"])
    Wo = np.asarray(inp["L0_attn_b_Wo"])
    Wg = np.asarray(inp["L0_attn_b_Wg"])
    Win = np.asarray(inp["L0_attn_b_Win"])
    dtb_b = np.asarray(inp["L0_attn_b_dt_bias"])
    alog_b = np.asarray(inp["L0_attn_b_A_log_bias"])
    gnw = np.asarray(inp["L0_attn_b_gnorm_w"])

    wqT = np.ascontiguousarray(wq.T)
    scT = np.ascontiguousarray(np.repeat(scales.T, 32, axis=0))
    WinT = np.ascontiguousarray(np.concatenate(
        [ipw[INTER:INTER + CONVD].T, ipw[INTER + CONVD:].T, ipw[:INTER].T], axis=1))
    WoutT = np.ascontiguousarray(wout.T)
    convbT = np.ascontiguousarray(convb.reshape(M_HBC, 128).T)
    Dexp = np.empty((128, 64), np.float32)
    Dexp[0:64, :] = Dv[0::2][None, :]
    Dexp[64:128, :] = Dv[1::2][None, :]
    normwc = np.ascontiguousarray(normw.reshape(64, 128).T)
    identf = np.eye(128, dtype=np.float32)
    mask01 = (np.arange(128)[None, :] >= np.arange(128)[:, None]).astype(np.float32)

    WinTp = np.zeros((DM, 128), np.float32)
    WinTp[:, 0:32] = Win.T
    W3T = np.ascontiguousarray(np.concatenate([Wq.T, Wk.T, Wv.T, Wg.T, WinTp], axis=1))
    WoT = np.ascontiguousarray(Wo.T)

    shared = {
        "wq_ffn": wqT, "scalesT": scT, "WinT": WinT, "WoutT": WoutT,
        "convw": np.ascontiguousarray(convw), "convbT": convbT,
        "dtb": np.ascontiguousarray(dtb.reshape(128, 1)),
        "alog": np.ascontiguousarray(alog.reshape(128, 1)),
        "DexpI": Dexp, "normwc": normwc, "identf": identf, "mask01I": mask01,
        "W3T": W3T, "WoT": WoT,
        "dtbbI": np.ascontiguousarray(dtb_b.reshape(32, 1)),
        "alogbI": np.ascontiguousarray(alog_b.reshape(32, 1)),
        "gnormI": np.ascontiguousarray(gnw.reshape(128, 1)),
    }
    invf = 1.0 / (10000.0 ** (np.arange(0, 128, 2) / 128.0))
    rope = []
    for half in range(2):
        ang = (half * T + np.arange(T))[:, None] * invf[None, :]
        cosT = np.cos(ang).astype(np.float32).T
        sinT = np.sin(ang).astype(np.float32).T
        rope.append((np.ascontiguousarray(np.concatenate([cosT, cosT], axis=0)),
                     np.ascontiguousarray(np.concatenate([-sinT, sinT], axis=0))))
    in_maps = []
    for c in range(8):
        seq = (c // 2) % 2
        half = c % 2
        t0 = half * T
        xs = x[seq, t0:t0 + T, :].T
        if half == 0:
            haloc = np.zeros((DM, HALO), np.float32)
        else:
            haloc = x[seq, t0 - HALO:t0, :].T
        xTc = np.ascontiguousarray(np.concatenate([haloc, xs], axis=1))
        m = dict(shared)
        m["xT"] = xTc
        m["corrmask"] = np.full((128, 1), float(half), np.float32)
        m["costabI"] = rope[half][0]
        m["sintabI"] = rope[half][1]
        in_maps.append(m)
    return in_maps


def kernel(**inp):
    in_maps = _prep_inputs(inp)
    nc = build()
    res = run_bass_kernel_spmd(nc, in_maps, core_ids=list(range(8)))
    outp = np.empty((2, 4096, 4096), np.float32)
    for c in range(4):
        seq = (c // 2) % 2
        half = c % 2
        outp[seq, half * T:(half + 1) * T, :] = res.results[c]["out"].T
    return outp
